# version 14
# baseline (speedup 1.0000x reference)
import numpy as np
import ml_dtypes
import concourse.bass as bass
import concourse.mybir as mybir
from concourse.bass_utils import run_bass_kernel_spmd
from contextlib import ExitStack

F32 = mybir.dt.float32
BF16 = mybir.dt.bfloat16
AF = mybir.ActivationFunctionType
OP = mybir.AluOpType
AX = mybir.AxisListType

D = 1024
DC = D // 128
NE = 8
HFF = 2048
NHC = HFF // 128
NH = 16
HD = 64
HG = 4
VB = 68
NXT = 3
ROPE_BASE = 10000.0
DEBUG_AVFM = False


def wtup(eng, g, key):
    t = g.get(key)
    if t is not None:
        eng.wait_ge(t[0], t[1])


class Sem:
    def __init__(self, ctx, nc, name):
        self.h = ctx.enter_context(nc.semaphore(name))
        self.n = 0

    def inc(self, instr, k=1):
        instr.then_inc(self.h, k)
        self.n += k
        return self.n

    def wait(self, eng, n=None):
        eng.wait_ge(self.h, self.n if n is None else n)


class Env:
    def __init__(self, nc, block, ctx):
        self.nc = nc
        self.block = block
        self.ctx = ctx
        self._sems = {}
        self.sPE = self.S("sPE")
        self.sACT = self.S("sACT")
        self.sDVE = self.S("sDVE")

    def S(self, name):
        if name not in self._sems:
            self._sems[name] = Sem(self.ctx, self.nc, name)
        return self._sems[name]

    def pe(self, fn):
        self.block.tensor(fn)

    def act(self, fn):
        self.block.scalar(fn)

    def dve(self, fn):
        self.block.vector(fn)

    def sync(self, fn):
        self.block.sync(fn)

    def gp(self, fn):
        self.block.gpsimd(fn)

    def barrier(self):
        tgts = [(s.h, s.n) for s in self._sems.values()]

        def mk(eng):
            for h, n in tgts:
                eng.wait_ge(h, n)

        for sec in (self.pe, self.act, self.dve, self.sync, self.gp):
            sec(mk)


def emit_ln_block(env, g, src_dram, row0, ntiles, ten, extra_cb=None):
    xn_fm, xt, xn_tm = ten["xn_fm"], ten["xt"], ten["xn_tm"]
    st6, mv, stdv, rstd, nm = ten["stats"]
    psum_tr, ident_bf, eps_ap = ten["psum_tr"], ten["ident_bf"], ten["eps_ap"]
    for p in range(0, ntiles, 2):
        pair = [p, p + 1] if p + 1 < ntiles else [p]
        pq = (p // 2) % 4

        def loads(sync, pair=pair):
            for j in pair:
                sync.wait_ge(env.sDVE.h, g.get(("xtfree", j % NXT), 0))
                wtup(sync, g, ("xtfreeO", j % NXT))
                sx = env.S(f"xt{j % NXT}")
                ins = sync.dma_start(
                    out=xt[j % NXT],
                    in_=src_dram[(row0 + j) * 128:(row0 + j + 1) * 128, :])
                sx.inc(ins, 16)
                g[("xtload", j % NXT)] = (sx.h, sx.n)
        env.sync(loads)

        def stats_f(v, pair=pair, pq=pq):
            v.wait_ge(env.sACT.h, g.get(("ln_act", pq), 0))
            for idx, j in enumerate(pair):
                wtup(v, g, ("xtload", j % NXT))
                so = 24 * (pq % 2) + 12 * idx
                v.bn_stats(out=st6[:, so:so + 6], in_=xt[j % NXT][:, 0:512])
                ins = v.bn_stats(out=st6[:, so + 6:so + 12],
                                 in_=xt[j % NXT][:, 512:1024])
                env.sDVE.inc(ins)
            v.wait_ge(env.sDVE.h, env.sDVE.n)
            for idx, j in enumerate(pair):
                so = 24 * (pq % 2) + 12 * idx
                ins = v.bn_aggr(out=mv[:, 4 * pq + 2 * idx:4 * pq + 2 * idx + 2],
                                in_=st6[:, so:so + 12])
                env.sDVE.inc(ins)
            g[("ln_stats", pq)] = env.sDVE.n
        env.dve(stats_f)

        def sqrt_f(s, pair=pair, pq=pq):
            s.wait_ge(env.sDVE.h, g[("ln_stats", pq)])
            ins = s.activation(out=stdv[:, 2 * pq:2 * pq + len(pair)],
                               in_=mv[:, 4 * pq + 1:4 * pq + 2 * len(pair):2],
                               func=AF.Sqrt, bias=eps_ap)
            env.sACT.inc(ins)
            g[("ln_act", pq)] = env.sACT.n
        env.act(sqrt_f)

        def norm_f(v, pair=pair, pq=pq, p=p):
            v.wait_ge(env.sACT.h, g[("ln_act", pq)])
            ins = v.reciprocal(out=rstd[:, 2 * pq:2 * pq + len(pair)],
                               in_=stdv[:, 2 * pq:2 * pq + len(pair)])
            env.sDVE.inc(ins)
            ins = v.tensor_scalar(out=nm[:, 2 * pq:2 * pq + len(pair)],
                                  in0=mv[:, 4 * pq:4 * pq + 2 * len(pair):2],
                                  scalar1=-1.0, scalar2=None, op0=OP.mult)
            env.sDVE.inc(ins)
            v.wait_ge(env.sDVE.h, env.sDVE.n)
            for idx, j in enumerate(pair):
                v.wait_ge(env.sPE.h, g.get(("xntm_war", j % 2), 0))
                ins = v.tensor_scalar(out=xn_tm[j % 2], in0=xt[j % NXT],
                                      scalar1=nm[:, 2 * pq + idx:2 * pq + idx + 1],
                                      scalar2=rstd[:, 2 * pq + idx:2 * pq + idx + 1],
                                      op0=OP.add, op1=OP.mult)
                env.sDVE.inc(ins)
                g[("xtfree", j % NXT)] = env.sDVE.n
                g[("xn_tm", j % 2)] = env.sDVE.n
        env.dve(norm_f)

        ntr = len(psum_tr)
        for j in pair:
            def tr_f(t, j=j):
                t.wait_ge(env.sDVE.h, g[("xn_tm", j % 2)])
                tf = g.get(("trfree", j % ntr), 0)
                if isinstance(tf, tuple):
                    t.wait_ge(env.sACT.h, tf[1])
                else:
                    t.wait_ge(env.sDVE.h, tf)
                for dc in range(DC):
                    ins = t.transpose(psum_tr[j % ntr][:, 128 * dc:128 * (dc + 1)],
                                      xn_tm[j % 2][:, 128 * dc:128 * (dc + 1)],
                                      ident_bf)
                env.sPE.inc(ins)
                g[("trdone", j % ntr)] = env.sPE.n
                g[("xntm_war", j % 2)] = env.sPE.n
            env.pe(tr_f)

            def trev_f(v, j=j):
                v.wait_ge(env.sPE.h, g[("trdone", j % ntr)])
                for dc in range(DC):
                    ins = v.tensor_copy(out=xn_fm[dc][:, 128 * j:128 * (j + 1)],
                                        in_=psum_tr[j % ntr][:, 128 * dc:128 * (dc + 1)])
                env.sDVE.inc(ins)
                g[("trfree", j % ntr)] = env.sDVE.n
            env.dve(trev_f)
        if extra_cb is not None:
            extra_cb(p)
    g["xn_ready"] = env.sDVE.n


def emit_gate(env, g, ntiles, ten):
    xn_fm, ones_bf = ten["xn_fm"], ten["ones_bf"]
    gate_wT_bf, gateb_bf = ten["gate_wT_bf"], ten["gateb_bf"]
    psum_misc, wsel = ten["psum_misc"], ten["wsel"]
    lgall, mx8all, e8all, sume, rsum = ten["gate_bufs"]

    for i in range(ntiles):
        def pe_i(t, i=i):
            if i == 0:
                t.wait_ge(env.sDVE.h, g["xn_ready"])
            else:
                t.wait_ge(env.sDVE.h, g[("gate_evac", i - 1)])
            for dc in range(DC):
                t.matmul(psum_misc[:, 0:8], xn_fm[dc][:, 128 * i:128 * (i + 1)],
                         gate_wT_bf[:, 8 * dc:8 * (dc + 1)],
                         start=(dc == 0), stop=False)
            ins = t.matmul(psum_misc[:, 0:8], ones_bf[0:1, 0:128],
                           gateb_bf[0:1, 0:8], start=False, stop=True)
            env.sPE.inc(ins)
            g[("gate_mm", i)] = env.sPE.n
        env.pe(pe_i)

        def dve_i(v, i=i):
            v.wait_ge(env.sPE.h, g[("gate_mm", i)])
            ins = v.tensor_copy(out=lgall[:, 8 * i:8 * (i + 1)], in_=psum_misc[:, 0:8])
            env.sDVE.inc(ins)
            g[("gate_evac", i)] = env.sDVE.n
        env.dve(dve_i)

    def dve_a(v):
        v.wait_ge(env.sDVE.h, g[("gate_evac", ntiles - 1)])
        for i in range(ntiles):
            ins = v.max(out=mx8all[:, 8 * i:8 * (i + 1)], in_=lgall[:, 8 * i:8 * (i + 1)])
        env.sDVE.inc(ins)
        v.wait_ge(env.sDVE.h, env.sDVE.n)
        ins = v.tensor_scalar(out=rsum[:, 0:ntiles], in0=mx8all[:, 0:8 * ntiles:8],
                              scalar1=-1.0, scalar2=None, op0=OP.mult)
        env.sDVE.inc(ins)
        g["gate_m1"] = env.sDVE.n
    env.dve(dve_a)

    def act_a(s):
        s.wait_ge(env.sDVE.h, g["gate_m1"])
        for i in range(ntiles):
            ins = s.activation(out=e8all[:, 8 * i:8 * (i + 1)],
                               in_=lgall[:, 8 * i:8 * (i + 1)], func=AF.Exp,
                               bias=rsum[:, i:i + 1], accum_out=sume[:, i:i + 1])
        env.sACT.inc(ins)
        g["gate_exp"] = env.sACT.n
    env.act(act_a)

    def dve_b(v, ntiles=ntiles):
        v.wait_ge(env.sACT.h, g["gate_exp"])
        ins = v.reciprocal(out=rsum[:, 0:ntiles], in_=sume[:, 0:ntiles])
        env.sDVE.inc(ins)
        for i in range(ntiles):
            ins = v.tensor_scalar(out=wsel[:, 8 * i:8 * (i + 1)],
                                  in0=lgall[:, 8 * i:8 * (i + 1)],
                                  scalar1=mx8all[:, 8 * i + 1:8 * i + 2], scalar2=None,
                                  op0=OP.is_ge)
        env.sDVE.inc(ins)
        v.wait_ge(env.sDVE.h, env.sDVE.n)
        ins = v.tensor_tensor(out=wsel[:, 0:8 * ntiles], in0=wsel[:, 0:8 * ntiles],
                              in1=e8all[:, 0:8 * ntiles], op=OP.mult)
        env.sDVE.inc(ins)
        v.wait_ge(env.sDVE.h, env.sDVE.n)
        for i in range(ntiles):
            ins = v.tensor_scalar(out=wsel[:, 8 * i:8 * (i + 1)],
                                  in0=wsel[:, 8 * i:8 * (i + 1)],
                                  scalar1=rsum[:, i:i + 1], scalar2=None, op0=OP.mult)
        env.sDVE.inc(ins)
        g["wsel_ready"] = env.sDVE.n
    env.dve(dve_b)

    wselbf, wselT = ten["wselbf"], ten["wselT"]
    for i in range(ntiles):
        def dve_c1(v, i=i):
            if i == 0:
                v.wait_ge(env.sDVE.h, g["wsel_ready"])
            v.wait_ge(env.sPE.h, g.get(("wbf_war", i % 2), 0))
            ins = v.tensor_copy(out=wselbf[i % 2], in_=wsel[:, 8 * i:8 * (i + 1)])
            env.sDVE.inc(ins)
            g[("wbf", i % 2)] = env.sDVE.n
        env.dve(dve_c1)

        def pe_c(t, i=i):
            t.wait_ge(env.sDVE.h, g[("wbf", i % 2)])
            t.wait_ge(env.sDVE.h, g.get(("trfree", i % 2), 0))
            ins = t.transpose(ten["psum_tr"][i % 2][0:8, 0:128], wselbf[i % 2],
                              ten["ident_bf"])
            env.sPE.inc(ins)
            g[("wbf_war", i % 2)] = env.sPE.n
            g[("wTt", i % 2)] = env.sPE.n
        env.pe(pe_c)

        def dve_c2(v, i=i):
            v.wait_ge(env.sPE.h, g[("wTt", i % 2)])
            ins = v.tensor_copy(out=wselT[0:8, 128 * i:128 * (i + 1)],
                                in_=ten["psum_tr"][i % 2][0:8, 0:128])
            env.sDVE.inc(ins)
            g[("trfree", i % 2)] = env.sDVE.n
            g["wselT_rdy"] = env.sDVE.n
        env.dve(dve_c2)


def _emit_b2mix_mm(env, t, g, i, ten, ntiles, dvebase, ilocal):
    if i >= ntiles:
        return
    for oh in range(2):
        if ilocal == 0:
            t.wait_ge(env.sDVE.h, g.get(("ogfree", oh), 0))
        else:
            t.wait_ge(env.sDVE.h, dvebase + 2 * ilocal)
        ins = t.matmul(ten["psum_og"][oh][:, 0:512],
                       ten["wselT"][0:8, 128 * i:128 * (i + 1)],
                       ten["b2bf"][0:8, 512 * oh:512 * (oh + 1)],
                       start=True, stop=True)
        env.sPE.inc(ins)
    g[("b2mm", i)] = env.sPE.n


def _emit_b2mix_evac(env, g, i, ten, ntiles):
    if i >= ntiles:
        return

    def dve_f(v, i=i):
        v.wait_ge(env.sPE.h, g[("b2mm", i)])
        for oh in range(2):
            col = 1024 * i + 512 * oh
            ins = v.tensor_copy(out=ten["accum"][:, col:col + 512],
                                in_=ten["psum_og"][oh][:, 0:512])
            env.sDVE.inc(ins)
            g[("ogfree", oh)] = env.sDVE.n
        g[("accum_init", i)] = env.sDVE.n
    env.dve(dve_f)


def emit_ffn_weight_load(env, g, e, w1_dram, w2_dram, b2_dram, ten):
    w1bf, w2bf = ten["w1bf"], ten["w2bf"]

    def gp_f(gp, e=e):
        sw1, sw2 = env.S("w1"), env.S("w2")
        gp.wait_ge(env.sPE.h, g.get("w1_inuse", 0))
        for dc in range(DC):
            ins = gp.dma_start(out=w1bf[dc],
                               in_=w1_dram[e, 128 * dc:128 * (dc + 1), :])
            sw1.inc(ins, 16)
        g["w1dma"] = (sw1.h, sw1.n)
        gp.wait_ge(env.sPE.h, g.get("w2_inuse", 0))
        for hc in range(NHC):
            ins = gp.dma_start(out=w2bf[hc],
                               in_=w2_dram[e, 128 * hc:128 * (hc + 1), :])
            sw2.inc(ins, 16)
        g["w2dma"] = (sw2.h, sw2.n)
    env.gp(gp_f)


def emit_ffn_block(env, g, row0, ntiles, src_dram, dst_dram, w1_dram, w2_dram,
                   b2_dram, ten):
    wbctx = g.pop("pending_wb", None)

    def wb_cb(p):
        if wbctx is not None:
            for i in (2 * (p // 2), 2 * (p // 2) + 1):
                if i < wbctx["ntiles"]:
                    emit_wb_tile(env, g, wbctx, i, ten)

    emit_ln_block(env, g, src_dram, row0, ntiles, ten, extra_cb=wb_cb)
    if wbctx is not None:
        for i in (2 * (ntiles // 2), 2 * (ntiles // 2) + 1):
            pass
    emit_gate(env, g, ntiles, ten)

    w1bf, w2bf = ten["w1bf"], ten["w2bf"]
    hsb, accum, wsel = ten["hsb"], ten["accum"], ten["wsel"]
    psum_h, psum_og = ten["psum_h"], ten["psum_og"]
    b1c, ones_bf, xn_fm, xt = ten["b1c"], ten["ones_bf"], ten["xn_fm"], ten["xt"]
    NA = (ntiles * 128) // 512

    for e in range(NE):
        w1_tgt, w2_tgt = g["w1dma"], g["w2dma"]

        for tj in range(NA):
            act_base = env.sACT.n
            b2_dvebase = env.sDVE.n

            def pe_a(t, e=e, tj=tj, w1_tgt=w1_tgt, act_base=act_base,
                     ntiles=ntiles, b2_dvebase=b2_dvebase):
                t.wait_ge(w1_tgt[0], w1_tgt[1])
                if e == 0:
                    t.wait_ge(env.sDVE.h, g["wselT_rdy"])
                for hc in range(NHC):
                    pb = hc % 3
                    if hc >= 3:
                        t.wait_ge(env.sACT.h, act_base + hc - 2)
                    else:
                        t.wait_ge(env.sACT.h, g.get(("hfree", pb), 0))
                    for dc in range(DC):
                        ins = t.matmul(psum_h[pb][:, 0:512],
                                       w1bf[dc][:, 128 * hc:128 * (hc + 1)],
                                       xn_fm[dc][:, 512 * tj:512 * (tj + 1)],
                                       start=(dc == 0), stop=(dc == DC - 1))
                    env.sPE.inc(ins)
                    g[("hmm", hc)] = env.sPE.n
                    if e == 0 and tj < 2 and hc % 2 == 1:
                        _emit_b2mix_mm(env, t, g, 8 * tj + hc // 2, ten, ntiles,
                                       b2_dvebase, hc // 2)
            env.pe(pe_a)
            g["w1_lastA"] = env.sPE.n
            if e == 0:
                if tj == 0:
                    for i in range(0, min(8, ntiles)):
                        _emit_b2mix_evac(env, g, i, ten, ntiles)
                elif tj == 1:
                    for i in range(8, ntiles):
                        _emit_b2mix_evac(env, g, i, ten, ntiles)

            def act_a(s, e=e):
                for hc in range(NHC):
                    pb = hc % 3
                    s.wait_ge(env.sPE.h, g[("hmm", hc)])
                    s.wait_ge(env.sPE.h, g.get("h_consumed", 0))
                    ins = s.activation(out=hsb[hc], in_=psum_h[pb], func=AF.Relu,
                                       bias=b1c[:, 16 * e + hc:16 * e + hc + 1])
                    env.sACT.inc(ins)
                    g[("hfree", pb)] = env.sACT.n
                g["hrdy"] = env.sACT.n
            env.act(act_a)
            hrdy = None

            dve_base = env.sDVE.n

            def pe_b(t, e=e, tj=tj, w2_tgt=w2_tgt, dve_base=dve_base):
                t.wait_ge(w2_tgt[0], w2_tgt[1])
                t.wait_ge(env.sACT.h, g["hrdy"])
                for t128 in range(4):
                    for oh in range(2):
                        k = t128 * 2 + oh
                        ob = k % 2
                        if k >= 2:
                            t.wait_ge(env.sDVE.h, dve_base + k - 1)
                        else:
                            t.wait_ge(env.sDVE.h, g.get(("ogfree", ob), 0))
                        for hc in range(NHC):
                            ins = t.matmul(psum_og[ob][:, 0:512],
                                           hsb[hc][:, 128 * t128:128 * (t128 + 1)],
                                           w2bf[hc][:, 512 * oh:512 * (oh + 1)],
                                           start=(hc == 0), stop=(hc == NHC - 1))
                        env.sPE.inc(ins)
                        g.setdefault("ogq", []).append(
                            (ob, env.sPE.n, 4 * tj + t128, oh))
                g["h_consumed"] = env.sPE.n
            env.pe(pe_b)
            g["w2_lastB"] = env.sPE.n

            def dve_bb(v, e=e, ntiles=ntiles):
                v.wait_ge(env.sDVE.h, g["wsel_ready"])
                if e == 0:
                    v.wait_ge(env.sDVE.h, g.get(("accum_init", ntiles - 1), 0))
                q = g.pop("ogq", [])
                for (ob, pecount, i, oh) in q:
                    v.wait_ge(env.sPE.h, pecount)
                    col = 1024 * i + 512 * oh
                    ins = v.scalar_tensor_tensor(
                        out=accum[:, col:col + 512], in0=psum_og[ob][:, 0:512],
                        scalar=wsel[:, 8 * i + e:8 * i + e + 1],
                        in1=accum[:, col:col + 512], op0=OP.mult, op1=OP.add)
                    env.sDVE.inc(ins)
                    g[("ogfree", ob)] = env.sDVE.n
                    g[("accum_last", i, oh)] = env.sDVE.n
            env.dve(dve_bb)

        if e + 1 < NE:
            g["w1_inuse"] = g["w1_lastA"]
            g["w2_inuse"] = g["w2_lastB"]
            emit_ffn_weight_load(env, g, e + 1, w1_dram, w2_dram, b2_dram, ten)

    wbctx = dict(src=src_dram, dst=dst_dram, row0=row0, ntiles=ntiles,
                 acc_keys={(i, oh): g[("accum_last", i, oh)]
                           for i in range(ntiles) for oh in range(2)})
    g["pending_wb"] = wbctx


def emit_wb_tile(env, g, wbctx, i, ten):
    xt, accum = ten["xt"], ten["accum"]
    row0, src_dram, dst_dram = wbctx["row0"], wbctx["src"], wbctx["dst"]

    def ld_i(sync, i=i):
        sync.wait_ge(env.sDVE.h, g.get(("xtfree", i % NXT), 0))
        wtup(sync, g, ("xtfreeO", i % NXT))
        sx = env.S(f"xt{i % NXT}")
        ins = sync.dma_start(
            out=xt[i % NXT],
            in_=src_dram[(row0 + i) * 128:(row0 + i + 1) * 128, :])
        sx.inc(ins, 16)
        g[("xtload", i % NXT)] = (sx.h, sx.n)
    env.sync(ld_i)

    def wb_i(v, i=i):
        wtup(v, g, ("xtload", i % NXT))
        v.wait_ge(env.sDVE.h, wbctx["acc_keys"][(i, 1)])
        v.wait_ge(env.sDVE.h, wbctx["acc_keys"][(i, 0)])
        ins = v.scalar_tensor_tensor(
            out=xt[i % NXT], in0=accum[:, 1024 * i:1024 * (i + 1)],
            scalar=0.5, in1=xt[i % NXT], op0=OP.mult, op1=OP.add)
        env.sDVE.inc(ins)
        g[("wb", i)] = env.sDVE.n
        g[("xtfree", i % NXT)] = env.sDVE.n
    env.dve(wb_i)

    def st_i(sync, i=i):
        sync.wait_ge(env.sDVE.h, g[("wb", i)])
        so = env.S(f"so{i % NXT}")
        ins = sync.dma_start(
            out=dst_dram[(row0 + i) * 128:(row0 + i + 1) * 128, :],
            in_=xt[i % NXT])
        so.inc(ins, 16)
        g[("xtfreeO", i % NXT)] = (so.h, so.n)
    env.sync(st_i)


def flush_wb(env, g, ten):
    wbctx = g.pop("pending_wb", None)
    if wbctx is not None:
        for i in range(wbctx["ntiles"]):
            emit_wb_tile(env, g, wbctx, i, ten)


def emit_ffn(env, g, T, src_dram, dst_dram, w1_dram, w2_dram, b2_dram, ten):
    nTB = (T + 2047) // 2048
    for tb in range(nTB):
        ntiles = min(16, T // 128 - tb * 16)
        emit_ffn_weight_load(env, g, 0, w1_dram, w2_dram, b2_dram, ten)
        emit_ffn_block(env, g, tb * 16, ntiles, src_dram, dst_dram,
                       w1_dram, w2_dram, b2_dram, ten)
        g["w1_inuse"] = env.sPE.n
        g["w2_inuse"] = env.sPE.n
    flush_wb(env, g, ten)


def emit_attn(env, g, S, N, src_dram, dst_dram, qkvwT_dram, projwT_dram, ten,
              qkvb_dram=None, dbg_d=None):
    nts = N // 128
    ntq = N // 512
    xn_fm, xt = ten["xn_fm"], ten["xt"]
    qk_lin = ten["qk_lin"]
    v_aug = ten["v_aug"]
    qf, kf = ten["qf"], ten["kf"]
    PTW = ten["PT"]
    avfm = ten["avfm"]
    qkvw_bf, projw_bf = ten["qkvw_bf"], ten["projw_bf"]
    stq = ten["psum_st"][0]
    psum_av, psum_rb = ten["psum_av"], ten["psum_rb"]
    psum_tr = ten["psum_tr"]
    ones_bf, qkvb_bf, projb_bf = ten["ones_bf"], ten["qkvb_bf"], ten["projb_bf"]
    cos4, sin4, ropet = ten["cos4"], ten["sin4"], ten["ropet"]
    rrow, rb_sb = ten["rrow"], ten["rb_sb"]

    def projw_f(gp):
        spw = env.S("pw")
        for dc in range(DC):
            ins = gp.dma_start(out=projw_bf[dc],
                               in_=projwT_dram[128 * dc:128 * (dc + 1), :])
            spw.inc(ins, 16)
        g["projw_dma"] = (spw.h, spw.n)
    env.gp(projw_f)

    for s_i in range(S):
        row0 = s_i * nts
        emit_ln_block(env, g, src_dram, row0, nts, ten)

        for hg in range(NH // HG):
            def qkvw_f(gp, hg=hg):
                sq = env.S("qkvw")
                gp.wait_ge(env.sPE.h, g.get("qkvw_inuse", 0))
                for dc in range(DC):
                    for pi, off in enumerate((0, D, 2 * D)):
                        ins = gp.dma_start(
                            out=qkvw_bf[dc][:, 256 * pi:256 * (pi + 1)],
                            in_=qkvwT_dram[128 * dc:128 * (dc + 1),
                                           off + 256 * hg:off + 256 * (hg + 1)])
                        sq.inc(ins, 16)
                for pi, off in enumerate((0, D, 2 * D)):
                    ins = gp.dma_start(
                        out=qkvb_bf[0:1, 256 * pi:256 * (pi + 1)],
                        in_=qkvb_dram[0:1, off + 256 * hg:off + 256 * (hg + 1)])
                    sq.inc(ins, 16)
                g["qkvw_dma"] = (sq.h, sq.n)
            env.gp(qkvw_f)

            for i in range(nts):
                def pe_q(t, i=i, hg=hg):
                    wtup(t, g, "qkvw_dma")
                    wtup(t, g, "cs")
                    if i == 0:
                        t.wait_ge(env.sDVE.h, g["xn_ready"])
                    for pi in range(3):
                        t.wait_ge(env.sDVE.h, g.get(("qkvps_free", pi), 0))
                        for dc in range(DC):
                            t.matmul(stq[:, 512 * pi:512 * pi + 256],
                                     xn_fm[dc][:, 128 * i:128 * (i + 1)],
                                     qkvw_bf[dc][:, 256 * pi:256 * (pi + 1)],
                                     start=(dc == 0), stop=False)
                        ins = t.matmul(
                            stq[:, 512 * pi:512 * pi + 256], ones_bf[0:1, 0:128],
                            qkvb_bf[0:1, 256 * pi:256 * (pi + 1)],
                            start=False, stop=True)
                        env.sPE.inc(ins)
                        g[("qkvmm", pi)] = env.sPE.n
                env.pe(pe_q)

                def dve_q(v, i=i):
                    v.wait_ge(env.sPE.h, g.get("qk_consumed", 0))
                    for pi in range(3):
                        v.wait_ge(env.sPE.h, g[("qkvmm", pi)])
                        if pi < 2:
                            ins = v.tensor_copy(out=qk_lin[i][:, 256 * pi:256 * (pi + 1)],
                                                in_=stq[:, 512 * pi:512 * pi + 256])
                        else:
                            dst = v_aug[i].rearrange("p (b c) -> p b c", c=VB)[:, :, 0:64]
                            src = stq[:, 512 * pi:512 * pi + 256].rearrange("p (b c) -> p b c", c=64)
                            ins = v.tensor_copy(out=dst, in_=src)
                        env.sDVE.inc(ins)
                        g[("qkvps_free", pi)] = env.sDVE.n
                    ins = v.memset(
                        v_aug[i].rearrange("p (b c) -> p b c", c=VB)[:, :, 64:65], 1.0)
                    env.sDVE.inc(ins)
                    g[("vaug_rdy", i)] = env.sDVE.n
                env.dve(dve_q)

            g["qkvw_inuse"] = env.sPE.n

            for i in range(nts):
                for off in (0, 256):
                    def dve_r(v, i=i, off=off):
                        tset = ropet[(2 * i + off // 256) % 3]
                        e_ap = qk_lin[i][:, off + 0:off + 256:2].rearrange(
                            "p (b c) -> p b c", c=32)
                        o_ap = qk_lin[i][:, off + 1:off + 256:2].rearrange(
                            "p (b c) -> p b c", c=32)
                        c_ap = cos4[:, 128 * i:128 * (i + 1)].rearrange(
                            "p (b c) -> p b c", c=32)
                        s_ap = sin4[:, 128 * i:128 * (i + 1)].rearrange(
                            "p (b c) -> p b c", c=32)
                        t3 = [tt.rearrange("p (b c) -> p b c", c=32) for tt in tset]
                        v.tensor_tensor(out=t3[0], in0=e_ap, in1=c_ap, op=OP.mult)
                        v.tensor_tensor(out=t3[1], in0=o_ap, in1=s_ap, op=OP.mult)
                        v.tensor_tensor(out=t3[2], in0=e_ap, in1=s_ap, op=OP.mult)
                        ins = v.tensor_tensor(out=t3[3], in0=o_ap, in1=c_ap, op=OP.mult)
                        env.sDVE.inc(ins)
                        v.wait_ge(env.sDVE.h, env.sDVE.n)
                        v.tensor_tensor(out=e_ap, in0=t3[0], in1=t3[1], op=OP.subtract)
                        ins = v.tensor_tensor(out=o_ap, in0=t3[2], in1=t3[3], op=OP.add)
                        env.sDVE.inc(ins)
                        g[("rope", i, off)] = env.sDVE.n
                    env.dve(dve_r)

            for i in range(nts):
                def pe_t(t, i=i):
                    t.wait_ge(env.sDVE.h, g[("rope", i, 0)])
                    t.wait_ge(env.sDVE.h, g[("rope", i, 256)])
                    tf = g.get(("trfree", 0), 0)
                    if isinstance(tf, tuple):
                        t.wait_ge(env.sACT.h, tf[1])
                    else:
                        t.wait_ge(env.sDVE.h, tf)
                    for half in range(2):
                        t.transpose(psum_tr[0][:, 256 * half:256 * half + 128],
                                    qk_lin[i][:, 128 * half:128 * (half + 1)],
                                    ten["ident_bf"])
                        ins = t.transpose(
                            psum_tr[0][:, 256 * half + 128:256 * half + 256],
                            qk_lin[i][:, 256 + 128 * half:256 + 128 * (half + 1)],
                            ten["ident_bf"])
                    env.sPE.inc(ins)
                    g[("atr", i)] = env.sPE.n
                env.pe(pe_t)

                def act_t(a, i=i):
                    a.wait_ge(env.sPE.h, g[("atr", i)])
                    a.wait_ge(env.sPE.h, g.get("qk_consumed", 0))
                    for half in range(2):
                        a.copy(out=qf[half][:, 128 * i:128 * (i + 1)],
                               in_=psum_tr[0][:, 256 * half:256 * half + 128])
                        ins = a.copy(
                            out=kf[half][:, 128 * i:128 * (i + 1)],
                            in_=psum_tr[0][:, 256 * half + 128:256 * half + 256])
                    env.sACT.inc(ins)
                    g[("trfree", 0)] = ("ACT", env.sACT.n)
                    g[("qkfm", i)] = env.sACT.n
                env.act(act_t)

            iters = [(hp, tq) for hp in range(HG // 2) for tq in range(ntq)]
            M = len(iters)
            for m in range(M + 2):
                if m < M:
                    g[("exp_base", m)] = env.sACT.n
                def pe_main(t, m=m, hg=hg):
                    if 0 <= m - 2 < M:
                        for hi in range(2):
                            t.wait_ge(env.sDVE.h, g[("rbfcast", m - 2)])
                            t.wait_ge(env.sDVE.h, g.get(("rbevac", m - 3), 0))
                            ins = t.matmul(psum_rb[64 * hi:64 * hi + 64, 0:512],
                                           ones_bf[0:1, 0:64],
                                           rrow[0:1, 1024 * ((m - 2) % 2) + 512 * hi:
                                                1024 * ((m - 2) % 2) + 512 * hi + 512],
                                           start=True, stop=True)
                        env.sPE.inc(ins)
                        g[("rbmm", m - 2)] = env.sPE.n
                    if 0 <= m - 1 < M:
                        hp1, tq1 = iters[m - 1]
                        for hi in range(2):
                            t.wait_ge(env.sDVE.h, g.get(("avfree", hi), 0))
                        t.wait_ge(env.sDVE.h, g[("vaug_rdy", nts - 1)])
                        for tk in range(nts):
                            for hi in range(2):
                                blkoff = VB * (2 * hp1 + hi)
                                t.wait_ge(env.sACT.h, g[("exp_base", m - 1)] + tk // 2 + 1)
                                ins = t.matmul(psum_av[hi][0:65, 0:512],
                                               v_aug[tk][:, blkoff:blkoff + 65],
                                               PTW[tk // 2][:, 1024 * (tk % 2) + 512 * hi:
                                                            1024 * (tk % 2) + 512 * hi + 512],
                                               start=(tk == 0), stop=(tk == nts - 1))
                        env.sPE.inc(ins)
                        g[("avmm", m - 1)] = env.sPE.n
                    if m < M:
                        hp0, tq0 = iters[m]
                        act_base = g[("exp_base", m)]
                        for tk in range(nts):
                            if m == 0:
                                t.wait_ge(env.sACT.h, g[("qkfm", max(tk, 3)) if ("qkfm", max(tk, 3)) in g else ("qkfm", nts - 1)])
                            for hi in range(2):
                                k = 2 * tk + hi
                                q = k % 4
                                if k >= 4:
                                    t.wait_ge(env.sACT.h, act_base + (k - 4) // 4 + 1)
                                else:
                                    t.wait_ge(env.sACT.h, act_base)
                                ro = 64 * hi
                                ins = t.matmul(
                                    stq[:, 512 * q:512 * (q + 1)],
                                    kf[hp0][ro:ro + 64, 128 * tk:128 * (tk + 1)],
                                    qf[hp0][ro:ro + 64, 512 * tq0:512 * (tq0 + 1)],
                                    start=True, stop=True)
                                env.sPE.inc(ins)
                                g[("stmm", m, tk)] = env.sPE.n
                env.pe(pe_main)

                if m < M:
                    def act_exp(s, m=m):
                        if m >= 1:
                            s.wait_ge(env.sPE.h, g[("avmm", m - 1)])
                        for j in range(nts // 2):
                            s.wait_ge(env.sPE.h, g[("stmm", m, 2 * j + 1)])
                            ins = s.activation(out=PTW[j], in_=stq[:, 0:2048],
                                               func=AF.Exp, scale=0.125)
                            env.sACT.inc(ins)
                    env.act(act_exp)

                def dve_norm(v, m=m, hg=hg):
                    if 0 <= m - 1 < M:
                        v.wait_ge(env.sPE.h, g[("avmm", m - 1)])
                        par = (m - 1) % 2
                        with env.nc.allow_low_precision(
                                reason="softmax normalizer in bf16 matches the "
                                       "bf16 P-matrix precision of this kernel"):
                            for hi in range(2):
                                ins = v.reciprocal(
                                    out=rrow[0:1, 1024 * par + 512 * hi:1024 * par + 512 * hi + 512],
                                    in_=psum_av[hi][64:65, 0:512])
                        env.sDVE.inc(ins)
                        g[("rbfcast", m - 1)] = env.sDVE.n
                        hp1, tq1 = iters[m - 1]
                        for hi in range(2):
                            hh = HG * hg + 2 * hp1 + hi
                            dst = avfm[hh // 2][64 * (hh % 2):64 * (hh % 2) + 64,
                                                512 * tq1:512 * (tq1 + 1)]
                            ins = v.tensor_copy(out=dst, in_=psum_av[hi][0:64, 0:512])
                            env.sDVE.inc(ins)
                            g[("avfree", hi)] = env.sDVE.n
                    if 0 <= m - 2 < M:
                        v.wait_ge(env.sPE.h, g[("rbmm", m - 2)])
                        v.wait_ge(env.sDVE.h, g.get(("rbmults", m - 3), 0))
                        ins = v.tensor_copy(out=rb_sb, in_=psum_rb[0:128, 0:512])
                        env.sDVE.inc(ins)
                        g[("rbevac", m - 2)] = env.sDVE.n
                        v.wait_ge(env.sDVE.h, env.sDVE.n)
                        hp2, tq2 = iters[m - 2]
                        for hi in range(2):
                            hh = HG * hg + 2 * hp2 + hi
                            dst = avfm[hh // 2][64 * (hh % 2):64 * (hh % 2) + 64,
                                                512 * tq2:512 * (tq2 + 1)]
                            ins = v.tensor_tensor(out=dst, in0=dst,
                                                  in1=rb_sb[64 * hi:64 * hi + 64, 0:512],
                                                  op=OP.mult)
                            env.sDVE.inc(ins)
                            g["avfm_last"] = env.sDVE.n
                        g[("rbmults", m - 2)] = env.sDVE.n
                env.dve(dve_norm)
            g["qk_consumed"] = env.sPE.n

        if dbg_d is not None and s_i == 0:
            dbgA, dbgB = dbg_d
            def dbg_f(gp):
                gp.wait_ge(env.sDVE.h, g["avfm_last"])
                sd = env.S("dbg")
                for j in range(DC):
                    ins = gp.dma_start(out=dbgA[128 * j:128 * (j + 1), :], in_=avfm[j])
                    sd.inc(ins, 16)
                ins = gp.dma_start(out=dbgB[0:128, :], in_=qf[0])
                sd.inc(ins, 16)
                ins = gp.dma_start(out=dbgB[128:256, :], in_=kf[0])
                sd.inc(ins, 16)
            env.gp(dbg_f)

        for i in range(nts):
            def ld_p(sync, i=i):
                sync.wait_ge(env.sDVE.h, g.get(("xtfree", i % NXT), 0))
                wtup(sync, g, ("xtfreeO", i % NXT))
                sx = env.S(f"xt{i % NXT}")
                ins = sync.dma_start(
                    out=xt[i % NXT],
                    in_=src_dram[(row0 + i) * 128:(row0 + i + 1) * 128, :])
                sx.inc(ins, 16)
                g[("xtload", i % NXT)] = (sx.h, sx.n)
            env.sync(ld_p)

            for oh in range(2):
                def pe_p(t, i=i, oh=oh):
                    if i == 0 and oh == 0:
                        t.wait_ge(env.sDVE.h, g["avfm_last"])
                        wtup(t, g, "projw_dma")
                        wtup(t, g, "cs")
                    pb = (2 * i + oh) % 2
                    t.wait_ge(env.sDVE.h, g.get(("pjfree", pb), 0))
                    t.wait_ge(env.sACT.h, g.get(("stfree", pb), 0))
                    for dc in range(DC):
                        t.matmul(stq[:, 512 * pb:512 * (pb + 1)],
                                 avfm[dc][:, 128 * i:128 * (i + 1)],
                                 projw_bf[dc][:, 512 * oh:512 * (oh + 1)],
                                 start=(dc == 0), stop=False)
                    ins = t.matmul(stq[:, 512 * pb:512 * (pb + 1)], ones_bf[0:1, 0:128],
                                   projb_bf[0:1, 512 * oh:512 * (oh + 1)],
                                   start=False, stop=True)
                    env.sPE.inc(ins)
                    g[("pjmm", pb)] = env.sPE.n
                env.pe(pe_p)

                def dve_p(v, i=i, oh=oh):
                    pb = (2 * i + oh) % 2
                    v.wait_ge(env.sPE.h, g[("pjmm", pb)])
                    wtup(v, g, ("xtload", i % NXT))
                    ins = v.tensor_tensor(
                        out=xt[i % NXT][:, 512 * oh:512 * (oh + 1)],
                        in0=stq[:, 512 * pb:512 * (pb + 1)],
                        in1=xt[i % NXT][:, 512 * oh:512 * (oh + 1)], op=OP.add)
                    env.sDVE.inc(ins)
                    g[("pjfree", pb)] = env.sDVE.n
                    g[("pwb", i)] = env.sDVE.n
                env.dve(dve_p)

            def st_p(sync, i=i):
                sync.wait_ge(env.sDVE.h, g[("pwb", i)])
                so = env.S(f"so{i % NXT}")
                ins = sync.dma_start(
                    out=dst_dram[(row0 + i) * 128:(row0 + i + 1) * 128, :],
                    in_=xt[i % NXT])
                so.inc(ins, 16)
                g[("xtfreeO", i % NXT)] = (so.h, so.n)
            env.sync(st_p)
            g[("xtfree", i % NXT)] = g[("pwb", i)]
        g["avfm_war"] = env.sPE.n
    g["qkvw_inuse"] = env.sPE.n


def build_program(S=2, N=2048, phases=("ffn1", "attn", "ffn2")):
    T = S * N
    nc = bass.Bass("TRN2", target_bir_lowering=False)

    x_in = nc.dram_tensor("x", [T, D], F32, kind="ExternalInput").ap()
    w1_d = nc.dram_tensor("w1", [NE, D, HFF], F32, kind="ExternalInput").ap()
    w2_d = nc.dram_tensor("w2", [NE, HFF, D], F32, kind="ExternalInput").ap()
    b1c_d = nc.dram_tensor("b1c", [128, 128], F32, kind="ExternalInput").ap()
    b2_d = nc.dram_tensor("b2", [NE, D], F32, kind="ExternalInput").ap()
    gwt_d = nc.dram_tensor("gwt", [128, 64], F32, kind="ExternalInput").ap()
    gb_d = nc.dram_tensor("gb", [1, NE], F32, kind="ExternalInput").ap()
    qkvwT_d = nc.dram_tensor("qkvwT", [D, 3 * D], F32, kind="ExternalInput").ap()
    qkvb_d = nc.dram_tensor("qkvb", [1, 3 * D], F32, kind="ExternalInput").ap()
    projwT_d = nc.dram_tensor("projwT", [D, D], F32, kind="ExternalInput").ap()
    projb_d = nc.dram_tensor("projb", [1, D], F32, kind="ExternalInput").ap()
    cos4_d = nc.dram_tensor("cos4", [128, N], F32, kind="ExternalInput").ap()
    sin4_d = nc.dram_tensor("sin4", [128, N], F32, kind="ExternalInput").ap()
    ones_d = nc.dram_tensor("onesd", [128, 128], F32, kind="ExternalInput").ap()
    ident_d = nc.dram_tensor("identd", [128, 128], F32, kind="ExternalInput").ap()
    out_d = nc.dram_tensor("out", [T, D], F32, kind="ExternalOutput").ap()
    dbg_d = nc.dram_tensor("dbg", [8 * 128, N], BF16, kind="ExternalOutput").ap() if DEBUG_AVFM else None
    dbg2_d = nc.dram_tensor("dbg2", [2 * 128, N], BF16, kind="ExternalOutput").ap() if DEBUG_AVFM else None
    x1_d = nc.dram_tensor("x1scratch", [T, D], F32).ap()
    x2_d = nc.dram_tensor("x2scratch", [T, D], F32).ap()

    with ExitStack() as ctx:
        E = ctx.enter_context

        def EA(cm):
            return E(cm).ap()

        xn_fm = [EA(nc.sbuf_tensor(f"xnfm{dc}", [128, min(N, 2048)], BF16)) for dc in range(DC)]
        xt = [EA(nc.sbuf_tensor(f"xt{j}", [128, D], F32)) for j in range(NXT)]
        xn_tm = [EA(nc.sbuf_tensor(f"xntm{j}", [128, D], BF16)) for j in range(2)]
        st6 = EA(nc.sbuf_tensor("st6", [128, 48], F32))
        mv = EA(nc.sbuf_tensor("mv", [128, 16], F32))
        stdv = EA(nc.sbuf_tensor("stdv", [128, 8], F32))
        rstd = EA(nc.sbuf_tensor("rstd", [128, 8], F32))
        nm = EA(nc.sbuf_tensor("nm", [128, 8], F32))
        epst = EA(nc.sbuf_tensor("epst", [128, 1], F32))
        ones_bf = EA(nc.sbuf_tensor("onesbf", [128, 128], BF16))
        ident_bf = EA(nc.sbuf_tensor("identbf", [128, 128], BF16))
        b1c = EA(nc.sbuf_tensor("b1csb", [128, 128], F32))
        gate_wT_bf = EA(nc.sbuf_tensor("gwtbf", [128, 64], BF16))
        gateb_bf = EA(nc.sbuf_tensor("gbbf", [1, NE], BF16))
        b2bf = EA(nc.sbuf_tensor("b2bf", [8, D], BF16))

        block = E(nc.Block())
        env = Env(nc, block, ctx)
        g = {}

        def setup_gp(gp):
            scg = env.S("cg")
            for dst, src in ((ones_bf, ones_d), (ident_bf, ident_d),
                             (gate_wT_bf, gwt_d), (gateb_bf, gb_d),
                             (b2bf, b2_d)):
                ins = gp.dma_start(out=dst, in_=src)
                scg.inc(ins, 16)
            g["consts"] = (scg.h, scg.n)
        env.gp(setup_gp)

        def setup_sync(sync):
            scx = env.S("cx")
            ins = sync.dma_start(out=b1c, in_=b1c_d)
            scx.inc(ins, 16)
            g["consts_x"] = (scx.h, scx.n)
        env.sync(setup_sync)

        def setup_dve(v):
            ins = v.memset(epst, 1e-5)
            env.sDVE.inc(ins)
        env.dve(setup_dve)

        cw, cx, cd = g["consts"], g["consts_x"], env.sDVE.n

        def wait_consts(e):
            e.wait_ge(cw[0], cw[1])
            e.wait_ge(cx[0], cx[1])
            e.wait_ge(env.sDVE.h, cd)
        for sec in (env.pe, env.act, env.dve):
            sec(wait_consts)

        base = {
            "xn_fm": xn_fm, "xt": xt, "xn_tm": xn_tm,
            "stats": (st6, mv, stdv, rstd, nm), "ident_bf": ident_bf,
            "eps_ap": epst, "ones_bf": ones_bf, "b1c": b1c,
            "gate_wT_bf": gate_wT_bf, "gateb_bf": gateb_bf, "b2bf": b2bf,
        }

        chain = []
        if "ffn1" in phases:
            dst = x1_d if ("attn" in phases or "ffn2" in phases) else out_d
            chain.append(("ffn", x_in, dst))
        if "attn" in phases:
            src = x1_d if "ffn1" in phases else x_in
            chain.append(("attn", src, x2_d if "ffn2" in phases else out_d))
        if "ffn2" in phases:
            src = x2_d if "attn" in phases else (x1_d if "ffn1" in phases else x_in)
            chain.append(("ffn", src, out_d))

        for ph_i, (kind, src, dst) in enumerate(chain):
            if kind == "ffn":
                with ExitStack() as fctx:
                    def FA(cm):
                        return fctx.enter_context(cm).ap()
                    ten = dict(base)
                    ten.update({
                        "w1bf": [FA(nc.sbuf_tensor(f"p{ph_i}w1bf{dc}", [128, HFF], BF16)) for dc in range(DC)],
                        "w2bf": [FA(nc.sbuf_tensor(f"p{ph_i}w2bf{hc}", [128, D], BF16)) for hc in range(NHC)],
                        "wselbf": [FA(nc.sbuf_tensor(f"p{ph_i}wbf{j}", [128, 8], BF16)) for j in range(2)],
                        "wselT": FA(nc.sbuf_tensor(f"p{ph_i}wselT", [8, 2048], BF16)),
                        "hsb": [FA(nc.sbuf_tensor(f"p{ph_i}hsb{hc}", [128, 512], BF16)) for hc in range(NHC)],
                        "accum": FA(nc.sbuf_tensor(f"p{ph_i}accum", [128, 16 * D], F32)),
                        "wsel": FA(nc.sbuf_tensor(f"p{ph_i}wsel", [128, 128], F32)),
                        "gate_bufs": (
                            FA(nc.sbuf_tensor(f"p{ph_i}lgall", [128, 128], F32)),
                            FA(nc.sbuf_tensor(f"p{ph_i}mx8all", [128, 128], F32)),
                            FA(nc.sbuf_tensor(f"p{ph_i}e8all", [128, 128], F32)),
                            FA(nc.sbuf_tensor(f"p{ph_i}sume", [128, 16], F32)),
                            FA(nc.sbuf_tensor(f"p{ph_i}rsum", [128, 16], F32)),
                        ),
                        "psum_h": [FA(nc.psum_tensor(f"p{ph_i}psh{j}", [128, 512], F32)) for j in range(3)],
                        "psum_og": [FA(nc.psum_tensor(f"p{ph_i}psog{j}", [128, 512], F32)) for j in range(2)],
                        "psum_tr": [FA(nc.psum_tensor(f"p{ph_i}pstr{j}", [128, 1024], BF16)) for j in range(2)],
                        "psum_misc": FA(nc.psum_tensor(f"p{ph_i}psmisc", [128, 512], F32)),
                    })
                    emit_ffn(env, g, T, src, dst, w1_d, w2_d, b2_d, ten)
                    env.barrier()
            else:
                with ExitStack() as actx:
                    def AA(cm):
                        return actx.enter_context(cm).ap()
                    nts = N // 128
                    ten = dict(base)
                    ten.update({
                        "qkvb_bf": AA(nc.sbuf_tensor("qkvbbf", [1, 768], BF16)),
                        "projb_bf": AA(nc.sbuf_tensor("projbbf", [1, D], BF16)),
                        "qk_lin": [AA(nc.sbuf_tensor(f"qkl{i}", [128, 512], BF16)) for i in range(nts)],
                        "v_aug": [AA(nc.sbuf_tensor(f"vaug{i}", [128, 4 * VB], BF16)) for i in range(nts)],
                        "qf": [AA(nc.sbuf_tensor(f"qf{j}", [128, N], BF16)) for j in range(2)],
                        "kf": [AA(nc.sbuf_tensor(f"kf{j}", [128, N], BF16)) for j in range(2)],
                        "PT": [AA(nc.sbuf_tensor(f"ptw{i}", [128, 2048], BF16)) for i in range(nts // 2)],
                        "avfm": [AA(nc.sbuf_tensor(f"avfm{j}", [128, N], BF16)) for j in range(DC)],
                        "qkvw_bf": [AA(nc.sbuf_tensor(f"qkvwbf{dc}", [128, 768], BF16)) for dc in range(DC)],
                        "projw_bf": [AA(nc.sbuf_tensor(f"pjwbf{dc}", [128, D], BF16)) for dc in range(DC)],
                        "cos4": AA(nc.sbuf_tensor("cos4sb", [128, N], BF16)),
                        "sin4": AA(nc.sbuf_tensor("sin4sb", [128, N], BF16)),
                        "ropet": [[AA(nc.sbuf_tensor(f"ropet{sj}_{k}", [128, 128], BF16)) for k in range(4)] for sj in range(3)],
                        "rrow": AA(nc.sbuf_tensor("rrow", [1, 2048], BF16)),
                        "rb_sb": AA(nc.sbuf_tensor("rbsb", [128, 512], F32)),
                        "psum_st": [AA(nc.psum_tensor("psstq", [128, 2048], F32))],
                        "psum_av": [AA(nc.psum_tensor(f"psav{hj}", [128, 512], F32)) for hj in range(2)],
                        "psum_rb": AA(nc.psum_tensor("psrb", [128, 512], F32)),
                        "psum_tr": [AA(nc.psum_tensor("psatr0", [128, 1024], BF16))],
                    })

                    def cs_f(gp):
                        scs = env.S("cs")
                        ins = gp.dma_start(out=ten["cos4"], in_=cos4_d)
                        scs.inc(ins, 16)
                        ins = gp.dma_start(out=ten["sin4"], in_=sin4_d)
                        scs.inc(ins, 16)
                        ins = gp.dma_start(out=ten["projb_bf"], in_=projb_d)
                        scs.inc(ins, 16)
                        g["cs"] = (scs.h, scs.n)
                    env.gp(cs_f)

                    def cs_wait(v):
                        wtup(v, g, "cs")
                    env.dve(cs_wait)

                    emit_attn(env, g, S, N, src, dst, qkvwT_d, projwT_d, ten,
                              qkvb_d, (dbg_d, dbg2_d) if dbg_d is not None else None)
                    env.barrier()

        env.barrier()

    return nc


def make_host_inputs(inp, S=2, N=2048, n_cores=8):
    x = np.asarray(inp["L0_attn_x"], dtype=np.float32)
    qkv_w = np.asarray(inp["L0_attn_qkv_w"], dtype=np.float32)
    qkv_b = np.asarray(inp["L0_attn_qkv_b"], dtype=np.float32)
    proj_w = np.asarray(inp["L0_attn_proj_w"], dtype=np.float32)
    proj_b = np.asarray(inp["L0_attn_proj_b"], dtype=np.float32)
    gate_w = np.asarray(inp["L0_ffn_gate_w"], dtype=np.float32)
    gate_b = np.asarray(inp["L0_ffn_gate_b"], dtype=np.float32)
    w1 = np.asarray(inp["L0_ffn_w1"], dtype=np.float32)
    b1 = np.asarray(inp["L0_ffn_b1"], dtype=np.float32)
    w2 = np.asarray(inp["L0_ffn_w2"], dtype=np.float32)
    b2 = np.asarray(inp["L0_ffn_b2"], dtype=np.float32)

    nts = N // 128
    pos = np.arange(N, dtype=np.float32)
    inv = (1.0 / ROPE_BASE ** (np.arange(0, HD, 2, dtype=np.float32) / np.float32(HD))).astype(np.float32)
    freqs = (pos[:, None] * inv[None, :]).astype(np.float32)
    cos4 = np.tile(np.cos(freqs).astype(np.float32), (1, 4))
    sin4 = np.tile(np.sin(freqs).astype(np.float32), (1, 4))
    cos4_l = np.ascontiguousarray(cos4.reshape(nts, 128, 128).transpose(1, 0, 2).reshape(128, N))
    sin4_l = np.ascontiguousarray(sin4.reshape(nts, 128, 128).transpose(1, 0, 2).reshape(128, N))

    common = dict(
        w1=np.ascontiguousarray(w1),
        w2=np.ascontiguousarray(w2),
        b1c=np.ascontiguousarray(b1.reshape(NE, NHC, 128).transpose(2, 0, 1).reshape(128, 128)),
        b2=np.ascontiguousarray(b2),
        gwt=np.ascontiguousarray(gate_w.T.reshape(NE, 128, NE).transpose(1, 0, 2).reshape(128, 64)),
        gb=np.ascontiguousarray(gate_b[None, :]),
        qkvwT=np.ascontiguousarray(qkv_w.T),
        qkvb=np.ascontiguousarray(qkv_b[None, :]),
        projwT=np.ascontiguousarray(proj_w.T),
        projb=np.ascontiguousarray(proj_b[None, :]),
        cos4=cos4_l,
        sin4=sin4_l,
        onesd=np.ones((128, 128), dtype=np.float32),
        identd=np.eye(128, dtype=np.float32),
    )
    maps = []
    B = x.shape[0]
    per = B // n_cores
    for c in range(n_cores):
        xs = np.ascontiguousarray(
            x[c * per:(c + 1) * per].reshape(per * x.shape[1], x.shape[2]))
        maps.append(dict(common, x=xs))
    return maps


def kernel(**inputs):
    S, N = 2, 2048
    nc = build_program(S=S, N=N)
    in_maps = make_host_inputs(inputs, S=S, N=N, n_cores=8)
    res = run_bass_kernel_spmd(nc, in_maps, core_ids=list(range(8)))
    outs = [res.results[c]["out"].reshape(S, N, D) for c in range(8)]
    return np.concatenate(outs, axis=0).astype(np.float32)


# version 15
# speedup vs baseline: 1.1391x; 1.1391x over previous
import numpy as np
import ml_dtypes
import concourse.bass as bass
import concourse.mybir as mybir
from concourse.bass_utils import run_bass_kernel_spmd
from contextlib import ExitStack

F32 = mybir.dt.float32
BF16 = mybir.dt.bfloat16
AF = mybir.ActivationFunctionType
OP = mybir.AluOpType
AX = mybir.AxisListType

D = 1024
DC = D // 128
NE = 8
HFF = 2048
NHC = HFF // 128
NH = 16
HD = 64
HG = 4
VB = 68
NXT = 4
ROPE_BASE = 10000.0
DEBUG_AVFM = False


def wtup(eng, g, key):
    t = g.get(key)
    if t is not None:
        eng.wait_ge(t[0], t[1])


class Sem:
    def __init__(self, ctx, nc, name):
        self.h = ctx.enter_context(nc.semaphore(name))
        self.n = 0

    def inc(self, instr, k=1):
        instr.then_inc(self.h, k)
        self.n += k
        return self.n

    def wait(self, eng, n=None):
        eng.wait_ge(self.h, self.n if n is None else n)


class Env:
    def __init__(self, nc, block, ctx):
        self.nc = nc
        self.block = block
        self.ctx = ctx
        self._sems = {}
        self.sPE = self.S("sPE")
        self.sACT = self.S("sACT")
        self.sDVE = self.S("sDVE")

    def S(self, name):
        if name not in self._sems:
            self._sems[name] = Sem(self.ctx, self.nc, name)
        return self._sems[name]

    def pe(self, fn):
        self.block.tensor(fn)

    def act(self, fn):
        self.block.scalar(fn)

    def dve(self, fn):
        self.block.vector(fn)

    def sync(self, fn):
        self.block.sync(fn)

    def gp(self, fn):
        self.block.gpsimd(fn)

    def barrier(self):
        tgts = [(s.h, s.n) for s in self._sems.values()]

        def mk(eng):
            for h, n in tgts:
                eng.wait_ge(h, n)

        for sec in (self.pe, self.act, self.dve, self.sync, self.gp):
            sec(mk)


def emit_ln_block(env, g, src_dram, row0, ntiles, ten, extra_cb=None):
    xn_fm, xt, xn_tm = ten["xn_fm"], ten["xt"], ten["xn_tm"]
    st6, mv, stdv, rstd, nm = ten["stats"]
    psum_tr, ident_bf, eps_ap = ten["psum_tr"], ten["ident_bf"], ten["eps_ap"]
    for p in range(0, ntiles, 2):
        pair = [p, p + 1] if p + 1 < ntiles else [p]
        pq = (p // 2) % 4

        def loads(sync, pair=pair):
            for j in pair:
                sync.wait_ge(env.sDVE.h, g.get(("xtfree", j % NXT), 0))
                wtup(sync, g, ("xtfreeO", j % NXT))
                sx = env.S(f"xt{j % NXT}")
                ins = sync.dma_start(
                    out=xt[j % NXT],
                    in_=src_dram[(row0 + j) * 128:(row0 + j + 1) * 128, :])
                sx.inc(ins, 16)
                g[("xtload", j % NXT)] = (sx.h, sx.n)
        env.sync(loads)

        def stats_f(v, pair=pair, pq=pq):
            v.wait_ge(env.sACT.h, g.get(("ln_act", pq), 0))
            for idx, j in enumerate(pair):
                wtup(v, g, ("xtload", j % NXT))
                so = 24 * (pq % 2) + 12 * idx
                v.bn_stats(out=st6[:, so:so + 6], in_=xt[j % NXT][:, 0:512])
                ins = v.bn_stats(out=st6[:, so + 6:so + 12],
                                 in_=xt[j % NXT][:, 512:1024])
                env.sDVE.inc(ins)
            v.wait_ge(env.sDVE.h, env.sDVE.n)
            for idx, j in enumerate(pair):
                so = 24 * (pq % 2) + 12 * idx
                ins = v.bn_aggr(out=mv[:, 4 * pq + 2 * idx:4 * pq + 2 * idx + 2],
                                in_=st6[:, so:so + 12])
                env.sDVE.inc(ins)
            g[("ln_stats", pq)] = env.sDVE.n
        env.dve(stats_f)

        def sqrt_f(s, pair=pair, pq=pq):
            s.wait_ge(env.sDVE.h, g[("ln_stats", pq)])
            ins = s.activation(out=stdv[:, 2 * pq:2 * pq + len(pair)],
                               in_=mv[:, 4 * pq + 1:4 * pq + 2 * len(pair):2],
                               func=AF.Sqrt, bias=eps_ap)
            env.sACT.inc(ins)
            g[("ln_act", pq)] = env.sACT.n
        env.act(sqrt_f)

        def norm_f(v, pair=pair, pq=pq, p=p):
            v.wait_ge(env.sACT.h, g[("ln_act", pq)])
            ins = v.reciprocal(out=rstd[:, 2 * pq:2 * pq + len(pair)],
                               in_=stdv[:, 2 * pq:2 * pq + len(pair)])
            env.sDVE.inc(ins)
            ins = v.tensor_scalar(out=nm[:, 2 * pq:2 * pq + len(pair)],
                                  in0=mv[:, 4 * pq:4 * pq + 2 * len(pair):2],
                                  scalar1=-1.0, scalar2=None, op0=OP.mult)
            env.sDVE.inc(ins)
            v.wait_ge(env.sDVE.h, env.sDVE.n)
            for idx, j in enumerate(pair):
                v.wait_ge(env.sPE.h, g.get(("xntm_war", j % 2), 0))
                ins = v.tensor_scalar(out=xn_tm[j % 2], in0=xt[j % NXT],
                                      scalar1=nm[:, 2 * pq + idx:2 * pq + idx + 1],
                                      scalar2=rstd[:, 2 * pq + idx:2 * pq + idx + 1],
                                      op0=OP.add, op1=OP.mult)
                env.sDVE.inc(ins)
                g[("xtfree", j % NXT)] = env.sDVE.n
                g[("xn_tm", j % 2)] = env.sDVE.n
        env.dve(norm_f)

        ntr = len(psum_tr)
        for j in pair:
            def tr_f(t, j=j):
                t.wait_ge(env.sDVE.h, g[("xn_tm", j % 2)])
                tf = g.get(("trfree", j % ntr), 0)
                if isinstance(tf, tuple):
                    t.wait_ge(env.sACT.h, tf[1])
                else:
                    t.wait_ge(env.sDVE.h, tf)
                for dc in range(DC):
                    ins = t.transpose(psum_tr[j % ntr][:, 128 * dc:128 * (dc + 1)],
                                      xn_tm[j % 2][:, 128 * dc:128 * (dc + 1)],
                                      ident_bf)
                env.sPE.inc(ins)
                g[("trdone", j % ntr)] = env.sPE.n
                g[("xntm_war", j % 2)] = env.sPE.n
            env.pe(tr_f)

            def trev_f(v, j=j):
                v.wait_ge(env.sPE.h, g[("trdone", j % ntr)])
                for dc in range(DC):
                    ins = v.tensor_copy(out=xn_fm[dc][:, 128 * j:128 * (j + 1)],
                                        in_=psum_tr[j % ntr][:, 128 * dc:128 * (dc + 1)])
                env.sDVE.inc(ins)
                g[("trfree", j % ntr)] = env.sDVE.n
            env.dve(trev_f)
        if extra_cb is not None:
            extra_cb(p)
    g["xn_ready"] = env.sDVE.n


def emit_gate(env, g, ntiles, ten):
    xn_fm, ones_bf = ten["xn_fm"], ten["ones_bf"]
    gate_wT_bf, gateb_bf = ten["gate_wT_bf"], ten["gateb_bf"]
    psum_misc, wsel = ten["psum_misc"], ten["wsel"]
    lgall, mx8all, e8all, sume, rsum = ten["gate_bufs"]

    for i in range(ntiles):
        def pe_i(t, i=i):
            if i == 0:
                t.wait_ge(env.sDVE.h, g["xn_ready"])
            else:
                t.wait_ge(env.sDVE.h, g[("gate_evac", i - 1)])
            for dc in range(DC):
                t.matmul(psum_misc[:, 0:8], xn_fm[dc][:, 128 * i:128 * (i + 1)],
                         gate_wT_bf[:, 8 * dc:8 * (dc + 1)],
                         start=(dc == 0), stop=False)
            ins = t.matmul(psum_misc[:, 0:8], ones_bf[0:1, 0:128],
                           gateb_bf[0:1, 0:8], start=False, stop=True)
            env.sPE.inc(ins)
            g[("gate_mm", i)] = env.sPE.n
        env.pe(pe_i)

        def dve_i(v, i=i):
            v.wait_ge(env.sPE.h, g[("gate_mm", i)])
            ins = v.tensor_copy(out=lgall[:, 8 * i:8 * (i + 1)], in_=psum_misc[:, 0:8])
            env.sDVE.inc(ins)
            g[("gate_evac", i)] = env.sDVE.n
        env.dve(dve_i)

    def dve_a(v):
        v.wait_ge(env.sDVE.h, g[("gate_evac", ntiles - 1)])
        for i in range(ntiles):
            ins = v.max(out=mx8all[:, 8 * i:8 * (i + 1)], in_=lgall[:, 8 * i:8 * (i + 1)])
        env.sDVE.inc(ins)
        v.wait_ge(env.sDVE.h, env.sDVE.n)
        ins = v.tensor_scalar(out=rsum[:, 0:ntiles], in0=mx8all[:, 0:8 * ntiles:8],
                              scalar1=-1.0, scalar2=None, op0=OP.mult)
        env.sDVE.inc(ins)
        g["gate_m1"] = env.sDVE.n
    env.dve(dve_a)

    def act_a(s):
        s.wait_ge(env.sDVE.h, g["gate_m1"])
        for i in range(ntiles):
            ins = s.activation(out=e8all[:, 8 * i:8 * (i + 1)],
                               in_=lgall[:, 8 * i:8 * (i + 1)], func=AF.Exp,
                               bias=rsum[:, i:i + 1], accum_out=sume[:, i:i + 1])
        env.sACT.inc(ins)
        g["gate_exp"] = env.sACT.n
    env.act(act_a)

    def dve_b(v, ntiles=ntiles):
        v.wait_ge(env.sACT.h, g["gate_exp"])
        ins = v.reciprocal(out=rsum[:, 0:ntiles], in_=sume[:, 0:ntiles])
        env.sDVE.inc(ins)
        for i in range(ntiles):
            ins = v.tensor_scalar(out=wsel[:, 8 * i:8 * (i + 1)],
                                  in0=lgall[:, 8 * i:8 * (i + 1)],
                                  scalar1=mx8all[:, 8 * i + 1:8 * i + 2], scalar2=None,
                                  op0=OP.is_ge)
        env.sDVE.inc(ins)
        v.wait_ge(env.sDVE.h, env.sDVE.n)
        ins = v.tensor_tensor(out=wsel[:, 0:8 * ntiles], in0=wsel[:, 0:8 * ntiles],
                              in1=e8all[:, 0:8 * ntiles], op=OP.mult)
        env.sDVE.inc(ins)
        v.wait_ge(env.sDVE.h, env.sDVE.n)
        for i in range(ntiles):
            ins = v.tensor_scalar(out=wsel[:, 8 * i:8 * (i + 1)],
                                  in0=wsel[:, 8 * i:8 * (i + 1)],
                                  scalar1=rsum[:, i:i + 1], scalar2=None, op0=OP.mult)
        env.sDVE.inc(ins)
        g["wsel_ready"] = env.sDVE.n
    env.dve(dve_b)

    wselbf, wselT = ten["wselbf"], ten["wselT"]
    for i in range(ntiles):
        def dve_c1(v, i=i):
            if i == 0:
                v.wait_ge(env.sDVE.h, g["wsel_ready"])
            v.wait_ge(env.sPE.h, g.get(("wbf_war", i % 2), 0))
            ins = v.tensor_copy(out=wselbf[i % 2], in_=wsel[:, 8 * i:8 * (i + 1)])
            env.sDVE.inc(ins)
            g[("wbf", i % 2)] = env.sDVE.n
        env.dve(dve_c1)

        def pe_c(t, i=i):
            t.wait_ge(env.sDVE.h, g[("wbf", i % 2)])
            t.wait_ge(env.sDVE.h, g.get(("trfree", i % 2), 0))
            ins = t.transpose(ten["psum_tr"][i % 2][0:8, 0:128], wselbf[i % 2],
                              ten["ident_bf"])
            env.sPE.inc(ins)
            g[("wbf_war", i % 2)] = env.sPE.n
            g[("wTt", i % 2)] = env.sPE.n
        env.pe(pe_c)

        def dve_c2(v, i=i):
            v.wait_ge(env.sPE.h, g[("wTt", i % 2)])
            ins = v.tensor_copy(out=wselT[0:8, 128 * i:128 * (i + 1)],
                                in_=ten["psum_tr"][i % 2][0:8, 0:128])
            env.sDVE.inc(ins)
            g[("trfree", i % 2)] = env.sDVE.n
            g["wselT_rdy"] = env.sDVE.n
        env.dve(dve_c2)


def _emit_b2mix_mm(env, t, g, i, ten, ntiles, dvebase, ilocal):
    if i >= ntiles:
        return
    for oh in range(2):
        if ilocal == 0:
            t.wait_ge(env.sDVE.h, g.get(("ogfree", oh), 0))
        else:
            t.wait_ge(env.sDVE.h, dvebase + 2 * ilocal)
        ins = t.matmul(ten["psum_og"][oh][:, 0:512],
                       ten["wselT"][0:8, 128 * i:128 * (i + 1)],
                       ten["b2bf"][0:8, 512 * oh:512 * (oh + 1)],
                       start=True, stop=True)
        env.sPE.inc(ins)
    g[("b2mm", i)] = env.sPE.n


def _emit_b2mix_evac(env, g, i, ten, ntiles):
    if i >= ntiles:
        return

    def dve_f(v, i=i):
        v.wait_ge(env.sPE.h, g[("b2mm", i)])
        for oh in range(2):
            col = 1024 * i + 512 * oh
            ins = v.tensor_copy(out=ten["accum"][:, col:col + 512],
                                in_=ten["psum_og"][oh][:, 0:512])
            env.sDVE.inc(ins)
            g[("ogfree", oh)] = env.sDVE.n
        g[("accum_init", i)] = env.sDVE.n
    env.dve(dve_f)


def emit_ffn_weight_load(env, g, e, w1_dram, w2_dram, b2_dram, ten):
    w1bf, w2bf = ten["w1bf"], ten["w2bf"]

    def gp_f(gp, e=e):
        sw1, sw2 = env.S("w1"), env.S("w2")
        gp.wait_ge(env.sPE.h, g.get("w1_inuse", 0))
        for dc in range(DC):
            ins = gp.dma_start(out=w1bf[dc],
                               in_=w1_dram[e, 128 * dc:128 * (dc + 1), :])
            sw1.inc(ins, 16)
        g["w1dma"] = (sw1.h, sw1.n)
        gp.wait_ge(env.sPE.h, g.get("w2_inuse", 0))
        for hc in range(NHC):
            ins = gp.dma_start(out=w2bf[hc],
                               in_=w2_dram[e, 128 * hc:128 * (hc + 1), :])
            sw2.inc(ins, 16)
        g["w2dma"] = (sw2.h, sw2.n)
    env.gp(gp_f)


def emit_ffn_block(env, g, row0, ntiles, src_dram, dst_dram, w1_dram, w2_dram,
                   b2_dram, ten):
    wbctx = g.pop("pending_wb", None)

    def wb_cb(p):
        if wbctx is not None:
            for i in (2 * (p // 2), 2 * (p // 2) + 1):
                if i < wbctx["ntiles"]:
                    emit_wb_tile(env, g, wbctx, i, ten)

    emit_ln_block(env, g, src_dram, row0, ntiles, ten, extra_cb=wb_cb)
    if wbctx is not None:
        for i in (2 * (ntiles // 2), 2 * (ntiles // 2) + 1):
            pass
    emit_gate(env, g, ntiles, ten)

    w1bf, w2bf = ten["w1bf"], ten["w2bf"]
    hsb, accum, wsel = ten["hsb"], ten["accum"], ten["wsel"]
    psum_h, psum_og = ten["psum_h"], ten["psum_og"]
    b1c, ones_bf, xn_fm, xt = ten["b1c"], ten["ones_bf"], ten["xn_fm"], ten["xt"]
    NA = (ntiles * 128) // 512

    for e in range(NE):
        w1_tgt, w2_tgt = g["w1dma"], g["w2dma"]

        for tj in range(NA):
            act_base = env.sACT.n
            b2_dvebase = env.sDVE.n

            def pe_a(t, e=e, tj=tj, w1_tgt=w1_tgt, act_base=act_base,
                     ntiles=ntiles, b2_dvebase=b2_dvebase):
                t.wait_ge(w1_tgt[0], w1_tgt[1])
                if e == 0:
                    t.wait_ge(env.sDVE.h, g["wselT_rdy"])
                for hc in range(NHC):
                    pb = hc % 3
                    if hc >= 3:
                        t.wait_ge(env.sACT.h, act_base + hc - 2)
                    else:
                        t.wait_ge(env.sACT.h, g.get(("hfree", pb), 0))
                    for dc in range(DC):
                        ins = t.matmul(psum_h[pb][:, 0:512],
                                       w1bf[dc][:, 128 * hc:128 * (hc + 1)],
                                       xn_fm[dc][:, 512 * tj:512 * (tj + 1)],
                                       start=(dc == 0), stop=(dc == DC - 1))
                    env.sPE.inc(ins)
                    g[("hmm", hc)] = env.sPE.n
                    if e == 0 and tj < 2 and hc % 2 == 1:
                        _emit_b2mix_mm(env, t, g, 8 * tj + hc // 2, ten, ntiles,
                                       b2_dvebase, hc // 2)
            env.pe(pe_a)
            g["w1_lastA"] = env.sPE.n
            if e == 0:
                if tj == 0:
                    for i in range(0, min(8, ntiles)):
                        _emit_b2mix_evac(env, g, i, ten, ntiles)
                elif tj == 1:
                    for i in range(8, ntiles):
                        _emit_b2mix_evac(env, g, i, ten, ntiles)

            def act_a(s, e=e):
                for hc in range(NHC):
                    pb = hc % 3
                    s.wait_ge(env.sPE.h, g[("hmm", hc)])
                    s.wait_ge(env.sPE.h, g.get("h_consumed", 0))
                    ins = s.activation(out=hsb[hc], in_=psum_h[pb], func=AF.Relu,
                                       bias=b1c[:, 16 * e + hc:16 * e + hc + 1])
                    env.sACT.inc(ins)
                    g[("hfree", pb)] = env.sACT.n
                g["hrdy"] = env.sACT.n
            env.act(act_a)
            hrdy = None

            dve_base = env.sDVE.n

            def pe_b(t, e=e, tj=tj, w2_tgt=w2_tgt, dve_base=dve_base):
                t.wait_ge(w2_tgt[0], w2_tgt[1])
                t.wait_ge(env.sACT.h, g["hrdy"])
                for t128 in range(4):
                    for oh in range(2):
                        k = t128 * 2 + oh
                        ob = k % 2
                        if k >= 2:
                            t.wait_ge(env.sDVE.h, dve_base + k - 1)
                        else:
                            t.wait_ge(env.sDVE.h, g.get(("ogfree", ob), 0))
                        for hc in range(NHC):
                            ins = t.matmul(psum_og[ob][:, 0:512],
                                           hsb[hc][:, 128 * t128:128 * (t128 + 1)],
                                           w2bf[hc][:, 512 * oh:512 * (oh + 1)],
                                           start=(hc == 0), stop=(hc == NHC - 1))
                        env.sPE.inc(ins)
                        g.setdefault("ogq", []).append(
                            (ob, env.sPE.n, 4 * tj + t128, oh))
                g["h_consumed"] = env.sPE.n
            env.pe(pe_b)
            g["w2_lastB"] = env.sPE.n

            def dve_bb(v, e=e, ntiles=ntiles):
                v.wait_ge(env.sDVE.h, g["wsel_ready"])
                if e == 0:
                    v.wait_ge(env.sDVE.h, g.get(("accum_init", ntiles - 1), 0))
                q = g.pop("ogq", [])
                for (ob, pecount, i, oh) in q:
                    v.wait_ge(env.sPE.h, pecount)
                    col = 1024 * i + 512 * oh
                    ins = v.scalar_tensor_tensor(
                        out=accum[:, col:col + 512], in0=psum_og[ob][:, 0:512],
                        scalar=wsel[:, 8 * i + e:8 * i + e + 1],
                        in1=accum[:, col:col + 512], op0=OP.mult, op1=OP.add)
                    env.sDVE.inc(ins)
                    g[("ogfree", ob)] = env.sDVE.n
                    g[("accum_last", i, oh)] = env.sDVE.n
            env.dve(dve_bb)

        if e + 1 < NE:
            g["w1_inuse"] = g["w1_lastA"]
            g["w2_inuse"] = g["w2_lastB"]
            emit_ffn_weight_load(env, g, e + 1, w1_dram, w2_dram, b2_dram, ten)

    wbctx = dict(src=src_dram, dst=dst_dram, row0=row0, ntiles=ntiles,
                 acc_keys={(i, oh): g[("accum_last", i, oh)]
                           for i in range(ntiles) for oh in range(2)})
    g["pending_wb"] = wbctx


def emit_wb_tile(env, g, wbctx, i, ten):
    xt, accum = ten["xt"], ten["accum"]
    row0, src_dram, dst_dram = wbctx["row0"], wbctx["src"], wbctx["dst"]

    def ld_i(sync, i=i):
        sync.wait_ge(env.sDVE.h, g.get(("xtfree", i % NXT), 0))
        wtup(sync, g, ("xtfreeO", i % NXT))
        sx = env.S(f"xt{i % NXT}")
        ins = sync.dma_start(
            out=xt[i % NXT],
            in_=src_dram[(row0 + i) * 128:(row0 + i + 1) * 128, :])
        sx.inc(ins, 16)
        g[("xtload", i % NXT)] = (sx.h, sx.n)
    env.sync(ld_i)

    def wb_i(v, i=i):
        wtup(v, g, ("xtload", i % NXT))
        v.wait_ge(env.sDVE.h, wbctx["acc_keys"][(i, 1)])
        v.wait_ge(env.sDVE.h, wbctx["acc_keys"][(i, 0)])
        ins = v.scalar_tensor_tensor(
            out=xt[i % NXT], in0=accum[:, 1024 * i:1024 * (i + 1)],
            scalar=0.5, in1=xt[i % NXT], op0=OP.mult, op1=OP.add)
        env.sDVE.inc(ins)
        g[("wb", i)] = env.sDVE.n
        g[("xtfree", i % NXT)] = env.sDVE.n
    env.dve(wb_i)

    def st_i(sync, i=i):
        sync.wait_ge(env.sDVE.h, g[("wb", i)])
        so = env.S(f"so{i % NXT}")
        ins = sync.dma_start(
            out=dst_dram[(row0 + i) * 128:(row0 + i + 1) * 128, :],
            in_=xt[i % NXT])
        so.inc(ins, 16)
        g[("xtfreeO", i % NXT)] = (so.h, so.n)
    env.sync(st_i)


def flush_wb(env, g, ten):
    wbctx = g.pop("pending_wb", None)
    if wbctx is not None:
        for i in range(wbctx["ntiles"]):
            emit_wb_tile(env, g, wbctx, i, ten)


def emit_ffn(env, g, T, src_dram, dst_dram, w1_dram, w2_dram, b2_dram, ten):
    nTB = (T + 2047) // 2048
    for tb in range(nTB):
        ntiles = min(16, T // 128 - tb * 16)
        emit_ffn_weight_load(env, g, 0, w1_dram, w2_dram, b2_dram, ten)
        emit_ffn_block(env, g, tb * 16, ntiles, src_dram, dst_dram,
                       w1_dram, w2_dram, b2_dram, ten)
        g["w1_inuse"] = env.sPE.n
        g["w2_inuse"] = env.sPE.n
    flush_wb(env, g, ten)


def emit_attn(env, g, S, N, src_dram, dst_dram, qkvwT_dram, projwT_dram, ten,
              qkvb_dram=None, dbg_d=None):
    nts = N // 128
    ntq = N // 512
    xn_fm, xt = ten["xn_fm"], ten["xt"]
    qk_lin = ten["qk_lin"]
    v_aug = ten["v_aug"]
    qf, kf = ten["qf"], ten["kf"]
    PTW = ten["PT"]
    avfm = ten["avfm"]
    qkvw_bf, projw_bf = ten["qkvw_bf"], ten["projw_bf"]
    stq = ten["psum_st"][0]
    psum_av, psum_rb = ten["psum_av"], ten["psum_rb"]
    psum_tr = ten["psum_tr"]
    ones_bf, qkvb_bf, projb_bf = ten["ones_bf"], ten["qkvb_bf"], ten["projb_bf"]
    cos4, sin4, ropet = ten["cos4"], ten["sin4"], ten["ropet"]
    rrow, rb_sb = ten["rrow"], ten["rb_sb"]

    def projw_f(gp):
        spw = env.S("pw")
        for dc in range(DC):
            ins = gp.dma_start(out=projw_bf[dc],
                               in_=projwT_dram[128 * dc:128 * (dc + 1), :])
            spw.inc(ins, 16)
        g["projw_dma"] = (spw.h, spw.n)
    env.gp(projw_f)

    for s_i in range(S):
        row0 = s_i * nts
        emit_ln_block(env, g, src_dram, row0, nts, ten)

        for hg in range(NH // HG):
            def qkvw_f(gp, hg=hg):
                sq = env.S("qkvw")
                gp.wait_ge(env.sPE.h, g.get("qkvw_inuse", 0))
                for dc in range(DC):
                    for pi, off in enumerate((0, D, 2 * D)):
                        ins = gp.dma_start(
                            out=qkvw_bf[dc][:, 256 * pi:256 * (pi + 1)],
                            in_=qkvwT_dram[128 * dc:128 * (dc + 1),
                                           off + 256 * hg:off + 256 * (hg + 1)])
                        sq.inc(ins, 16)
                for pi, off in enumerate((0, D, 2 * D)):
                    ins = gp.dma_start(
                        out=qkvb_bf[0:1, 256 * pi:256 * (pi + 1)],
                        in_=qkvb_dram[0:1, off + 256 * hg:off + 256 * (hg + 1)])
                    sq.inc(ins, 16)
                g["qkvw_dma"] = (sq.h, sq.n)
            env.gp(qkvw_f)

            for i in range(nts):
                def pe_q(t, i=i, hg=hg):
                    wtup(t, g, "qkvw_dma")
                    wtup(t, g, "cs")
                    if i == 0:
                        t.wait_ge(env.sDVE.h, g["xn_ready"])
                    for pi in range(3):
                        t.wait_ge(env.sDVE.h, g.get(("qkvps_free", pi), 0))
                        for dc in range(DC):
                            t.matmul(stq[:, 512 * pi:512 * pi + 256],
                                     xn_fm[dc][:, 128 * i:128 * (i + 1)],
                                     qkvw_bf[dc][:, 256 * pi:256 * (pi + 1)],
                                     start=(dc == 0), stop=False)
                        ins = t.matmul(
                            stq[:, 512 * pi:512 * pi + 256], ones_bf[0:1, 0:128],
                            qkvb_bf[0:1, 256 * pi:256 * (pi + 1)],
                            start=False, stop=True)
                        env.sPE.inc(ins)
                        g[("qkvmm", pi)] = env.sPE.n
                env.pe(pe_q)

                def dve_q(v, i=i):
                    v.wait_ge(env.sPE.h, g.get("qk_consumed", 0))
                    for pi in range(3):
                        v.wait_ge(env.sPE.h, g[("qkvmm", pi)])
                        if pi < 2:
                            ins = v.tensor_copy(out=qk_lin[i][:, 256 * pi:256 * (pi + 1)],
                                                in_=stq[:, 512 * pi:512 * pi + 256])
                        else:
                            dst = v_aug[i].rearrange("p (b c) -> p b c", c=VB)[:, :, 0:64]
                            src = stq[:, 512 * pi:512 * pi + 256].rearrange("p (b c) -> p b c", c=64)
                            ins = v.tensor_copy(out=dst, in_=src)
                        env.sDVE.inc(ins)
                        g[("qkvps_free", pi)] = env.sDVE.n
                    ins = v.memset(
                        v_aug[i].rearrange("p (b c) -> p b c", c=VB)[:, :, 64:65], 1.0)
                    env.sDVE.inc(ins)
                    g[("vaug_rdy", i)] = env.sDVE.n
                env.dve(dve_q)

            g["qkvw_inuse"] = env.sPE.n

            for i in range(nts):
                for off in (0, 256):
                    def dve_r(v, i=i, off=off):
                        tset = ropet[(2 * i + off // 256) % 3]
                        e_ap = qk_lin[i][:, off + 0:off + 256:2].rearrange(
                            "p (b c) -> p b c", c=32)
                        o_ap = qk_lin[i][:, off + 1:off + 256:2].rearrange(
                            "p (b c) -> p b c", c=32)
                        c_ap = cos4[:, 128 * i:128 * (i + 1)].rearrange(
                            "p (b c) -> p b c", c=32)
                        s_ap = sin4[:, 128 * i:128 * (i + 1)].rearrange(
                            "p (b c) -> p b c", c=32)
                        t3 = [tt.rearrange("p (b c) -> p b c", c=32) for tt in tset]
                        v.tensor_tensor(out=t3[0], in0=e_ap, in1=c_ap, op=OP.mult)
                        v.tensor_tensor(out=t3[1], in0=o_ap, in1=s_ap, op=OP.mult)
                        v.tensor_tensor(out=t3[2], in0=e_ap, in1=s_ap, op=OP.mult)
                        ins = v.tensor_tensor(out=t3[3], in0=o_ap, in1=c_ap, op=OP.mult)
                        env.sDVE.inc(ins)
                        v.wait_ge(env.sDVE.h, env.sDVE.n)
                        v.tensor_tensor(out=e_ap, in0=t3[0], in1=t3[1], op=OP.subtract)
                        ins = v.tensor_tensor(out=o_ap, in0=t3[2], in1=t3[3], op=OP.add)
                        env.sDVE.inc(ins)
                        g[("rope", i, off)] = env.sDVE.n
                    env.dve(dve_r)

            for i in range(nts):
                def pe_t(t, i=i):
                    t.wait_ge(env.sDVE.h, g[("rope", i, 0)])
                    t.wait_ge(env.sDVE.h, g[("rope", i, 256)])
                    tf = g.get(("trfree", 0), 0)
                    if isinstance(tf, tuple):
                        t.wait_ge(env.sACT.h, tf[1])
                    else:
                        t.wait_ge(env.sDVE.h, tf)
                    for half in range(2):
                        t.transpose(psum_tr[0][:, 256 * half:256 * half + 128],
                                    qk_lin[i][:, 128 * half:128 * (half + 1)],
                                    ten["ident_bf"])
                        ins = t.transpose(
                            psum_tr[0][:, 256 * half + 128:256 * half + 256],
                            qk_lin[i][:, 256 + 128 * half:256 + 128 * (half + 1)],
                            ten["ident_bf"])
                    env.sPE.inc(ins)
                    g[("atr", i)] = env.sPE.n
                env.pe(pe_t)

                def act_t(a, i=i):
                    a.wait_ge(env.sPE.h, g[("atr", i)])
                    a.wait_ge(env.sPE.h, g.get("qk_consumed", 0))
                    for half in range(2):
                        a.copy(out=qf[half][:, 128 * i:128 * (i + 1)],
                               in_=psum_tr[0][:, 256 * half:256 * half + 128])
                        ins = a.copy(
                            out=kf[half][:, 128 * i:128 * (i + 1)],
                            in_=psum_tr[0][:, 256 * half + 128:256 * half + 256])
                    env.sACT.inc(ins)
                    g[("trfree", 0)] = ("ACT", env.sACT.n)
                    g[("qkfm", i)] = env.sACT.n
                env.act(act_t)

            iters = [(hp, tq) for hp in range(HG // 2) for tq in range(ntq)]
            M = len(iters)
            for m in range(M + 2):
                if m < M:
                    g[("exp_base", m)] = env.sACT.n
                def pe_main(t, m=m, hg=hg):
                    if 0 <= m - 2 < M:
                        for hi in range(2):
                            t.wait_ge(env.sDVE.h, g[("rbfcast", m - 2)])
                            t.wait_ge(env.sDVE.h, g.get(("rbevac", m - 3), 0))
                            ins = t.matmul(psum_rb[64 * hi:64 * hi + 64, 0:512],
                                           ones_bf[0:1, 0:64],
                                           rrow[0:1, 1024 * ((m - 2) % 2) + 512 * hi:
                                                1024 * ((m - 2) % 2) + 512 * hi + 512],
                                           start=True, stop=True)
                        env.sPE.inc(ins)
                        g[("rbmm", m - 2)] = env.sPE.n
                    if 0 <= m - 1 < M:
                        hp1, tq1 = iters[m - 1]
                        for hi in range(2):
                            t.wait_ge(env.sDVE.h, g.get(("avfree", hi), 0))
                        t.wait_ge(env.sDVE.h, g[("vaug_rdy", nts - 1)])
                        for tk in range(nts):
                            for hi in range(2):
                                blkoff = VB * (2 * hp1 + hi)
                                t.wait_ge(env.sACT.h, g[("exp_base", m - 1)] + tk + 1)
                                ins = t.matmul(psum_av[hi][0:65, 0:512],
                                               v_aug[tk][:, blkoff:blkoff + 65],
                                               PTW[tk][:, 512 * hi:512 * hi + 512],
                                               start=(tk == 0), stop=(tk == nts - 1))
                        env.sPE.inc(ins)
                        g[("avmm", m - 1)] = env.sPE.n
                    if m < M:
                        hp0, tq0 = iters[m]
                        act_base = g[("exp_base", m)]
                        for tk in range(nts):
                            if m == 0:
                                t.wait_ge(env.sACT.h, g[("qkfm", max(tk, 3)) if ("qkfm", max(tk, 3)) in g else ("qkfm", nts - 1)])
                            for hi in range(2):
                                k = 2 * tk + hi
                                q = k % 4
                                if k >= 4:
                                    t.wait_ge(env.sACT.h, act_base + (k - 4) // 2 + 1)
                                else:
                                    t.wait_ge(env.sACT.h, act_base)
                                ro = 64 * hi
                                ins = t.matmul(
                                    stq[:, 512 * q:512 * (q + 1)],
                                    kf[hp0][ro:ro + 64, 128 * tk:128 * (tk + 1)],
                                    qf[hp0][ro:ro + 64, 512 * tq0:512 * (tq0 + 1)],
                                    start=True, stop=True)
                                env.sPE.inc(ins)
                                g[("stmm", m, tk)] = env.sPE.n
                env.pe(pe_main)

                if m < M:
                    def act_exp(s, m=m):
                        for tk in range(nts):
                            s.wait_ge(env.sPE.h, g[("stmm", m, tk)])
                            if m >= 1:
                                s.wait_ge(env.sPE.h, g[("avmm", m - 1)])
                            ins = s.activation(out=PTW[tk],
                                               in_=stq[:, 1024 * (tk % 2):1024 * (tk % 2) + 1024],
                                               func=AF.Exp, scale=0.125)
                            env.sACT.inc(ins)
                    env.act(act_exp)

                def dve_norm(v, m=m, hg=hg):
                    if 0 <= m - 1 < M:
                        v.wait_ge(env.sPE.h, g[("avmm", m - 1)])
                        par = (m - 1) % 2
                        with env.nc.allow_low_precision(
                                reason="softmax normalizer in bf16 matches the "
                                       "bf16 P-matrix precision of this kernel"):
                            for hi in range(2):
                                ins = v.reciprocal(
                                    out=rrow[0:1, 1024 * par + 512 * hi:1024 * par + 512 * hi + 512],
                                    in_=psum_av[hi][64:65, 0:512])
                        env.sDVE.inc(ins)
                        g[("rbfcast", m - 1)] = env.sDVE.n
                        hp1, tq1 = iters[m - 1]
                        for hi in range(2):
                            hh = HG * hg + 2 * hp1 + hi
                            dst = avfm[hh // 2][64 * (hh % 2):64 * (hh % 2) + 64,
                                                512 * tq1:512 * (tq1 + 1)]
                            ins = v.tensor_copy(out=dst, in_=psum_av[hi][0:64, 0:512])
                            env.sDVE.inc(ins)
                            g[("avfree", hi)] = env.sDVE.n
                    if 0 <= m - 2 < M:
                        v.wait_ge(env.sPE.h, g[("rbmm", m - 2)])
                        v.wait_ge(env.sDVE.h, g.get(("rbmults", m - 3), 0))
                        ins = v.tensor_copy(out=rb_sb, in_=psum_rb[0:128, 0:512])
                        env.sDVE.inc(ins)
                        g[("rbevac", m - 2)] = env.sDVE.n
                        v.wait_ge(env.sDVE.h, env.sDVE.n)
                        hp2, tq2 = iters[m - 2]
                        for hi in range(2):
                            hh = HG * hg + 2 * hp2 + hi
                            dst = avfm[hh // 2][64 * (hh % 2):64 * (hh % 2) + 64,
                                                512 * tq2:512 * (tq2 + 1)]
                            ins = v.tensor_tensor(out=dst, in0=dst,
                                                  in1=rb_sb[64 * hi:64 * hi + 64, 0:512],
                                                  op=OP.mult)
                            env.sDVE.inc(ins)
                            g["avfm_last"] = env.sDVE.n
                        g[("rbmults", m - 2)] = env.sDVE.n
                env.dve(dve_norm)
            g["qk_consumed"] = env.sPE.n

        if dbg_d is not None and s_i == 0:
            dbgA, dbgB = dbg_d
            def dbg_f(gp):
                gp.wait_ge(env.sDVE.h, g["avfm_last"])
                sd = env.S("dbg")
                for j in range(DC):
                    ins = gp.dma_start(out=dbgA[128 * j:128 * (j + 1), :], in_=avfm[j])
                    sd.inc(ins, 16)
                ins = gp.dma_start(out=dbgB[0:128, :], in_=qf[0])
                sd.inc(ins, 16)
                ins = gp.dma_start(out=dbgB[128:256, :], in_=kf[0])
                sd.inc(ins, 16)
            env.gp(dbg_f)

        for i in range(nts):
            def ld_p(sync, i=i):
                sync.wait_ge(env.sDVE.h, g.get(("xtfree", i % NXT), 0))
                wtup(sync, g, ("xtfreeO", i % NXT))
                sx = env.S(f"xt{i % NXT}")
                ins = sync.dma_start(
                    out=xt[i % NXT],
                    in_=src_dram[(row0 + i) * 128:(row0 + i + 1) * 128, :])
                sx.inc(ins, 16)
                g[("xtload", i % NXT)] = (sx.h, sx.n)
            env.sync(ld_p)

            for oh in range(2):
                def pe_p(t, i=i, oh=oh):
                    if i == 0 and oh == 0:
                        t.wait_ge(env.sDVE.h, g["avfm_last"])
                        wtup(t, g, "projw_dma")
                        wtup(t, g, "cs")
                    pb = (2 * i + oh) % 2
                    t.wait_ge(env.sDVE.h, g.get(("pjfree", pb), 0))
                    t.wait_ge(env.sACT.h, g.get(("stfree", pb), 0))
                    for dc in range(DC):
                        t.matmul(stq[:, 512 * pb:512 * (pb + 1)],
                                 avfm[dc][:, 128 * i:128 * (i + 1)],
                                 projw_bf[dc][:, 512 * oh:512 * (oh + 1)],
                                 start=(dc == 0), stop=False)
                    ins = t.matmul(stq[:, 512 * pb:512 * (pb + 1)], ones_bf[0:1, 0:128],
                                   projb_bf[0:1, 512 * oh:512 * (oh + 1)],
                                   start=False, stop=True)
                    env.sPE.inc(ins)
                    g[("pjmm", pb)] = env.sPE.n
                env.pe(pe_p)

                def dve_p(v, i=i, oh=oh):
                    pb = (2 * i + oh) % 2
                    v.wait_ge(env.sPE.h, g[("pjmm", pb)])
                    wtup(v, g, ("xtload", i % NXT))
                    ins = v.tensor_tensor(
                        out=xt[i % NXT][:, 512 * oh:512 * (oh + 1)],
                        in0=stq[:, 512 * pb:512 * (pb + 1)],
                        in1=xt[i % NXT][:, 512 * oh:512 * (oh + 1)], op=OP.add)
                    env.sDVE.inc(ins)
                    g[("pjfree", pb)] = env.sDVE.n
                    g[("pwb", i)] = env.sDVE.n
                env.dve(dve_p)

            def st_p(sync, i=i):
                sync.wait_ge(env.sDVE.h, g[("pwb", i)])
                so = env.S(f"so{i % NXT}")
                ins = sync.dma_start(
                    out=dst_dram[(row0 + i) * 128:(row0 + i + 1) * 128, :],
                    in_=xt[i % NXT])
                so.inc(ins, 16)
                g[("xtfreeO", i % NXT)] = (so.h, so.n)
            env.sync(st_p)
            g[("xtfree", i % NXT)] = g[("pwb", i)]
        g["avfm_war"] = env.sPE.n
    g["qkvw_inuse"] = env.sPE.n


def build_program(S=2, N=2048, phases=("ffn1", "attn", "ffn2")):
    T = S * N
    nc = bass.Bass("TRN2", target_bir_lowering=False)

    x_in = nc.dram_tensor("x", [T, D], F32, kind="ExternalInput").ap()
    w1_d = nc.dram_tensor("w1", [NE, D, HFF], F32, kind="ExternalInput").ap()
    w2_d = nc.dram_tensor("w2", [NE, HFF, D], F32, kind="ExternalInput").ap()
    b1c_d = nc.dram_tensor("b1c", [128, 128], F32, kind="ExternalInput").ap()
    b2_d = nc.dram_tensor("b2", [NE, D], F32, kind="ExternalInput").ap()
    gwt_d = nc.dram_tensor("gwt", [128, 64], F32, kind="ExternalInput").ap()
    gb_d = nc.dram_tensor("gb", [1, NE], F32, kind="ExternalInput").ap()
    qkvwT_d = nc.dram_tensor("qkvwT", [D, 3 * D], F32, kind="ExternalInput").ap()
    qkvb_d = nc.dram_tensor("qkvb", [1, 3 * D], F32, kind="ExternalInput").ap()
    projwT_d = nc.dram_tensor("projwT", [D, D], F32, kind="ExternalInput").ap()
    projb_d = nc.dram_tensor("projb", [1, D], F32, kind="ExternalInput").ap()
    cos4_d = nc.dram_tensor("cos4", [128, N], F32, kind="ExternalInput").ap()
    sin4_d = nc.dram_tensor("sin4", [128, N], F32, kind="ExternalInput").ap()
    ones_d = nc.dram_tensor("onesd", [128, 128], F32, kind="ExternalInput").ap()
    ident_d = nc.dram_tensor("identd", [128, 128], F32, kind="ExternalInput").ap()
    out_d = nc.dram_tensor("out", [T, D], F32, kind="ExternalOutput").ap()
    dbg_d = nc.dram_tensor("dbg", [8 * 128, N], BF16, kind="ExternalOutput").ap() if DEBUG_AVFM else None
    dbg2_d = nc.dram_tensor("dbg2", [2 * 128, N], BF16, kind="ExternalOutput").ap() if DEBUG_AVFM else None
    x1_d = nc.dram_tensor("x1scratch", [T, D], F32).ap()
    x2_d = nc.dram_tensor("x2scratch", [T, D], F32).ap()

    with ExitStack() as ctx:
        E = ctx.enter_context

        def EA(cm):
            return E(cm).ap()

        xn_fm = [EA(nc.sbuf_tensor(f"xnfm{dc}", [128, min(N, 2048)], BF16)) for dc in range(DC)]
        xt = [EA(nc.sbuf_tensor(f"xt{j}", [128, D], F32)) for j in range(NXT)]
        xn_tm = [EA(nc.sbuf_tensor(f"xntm{j}", [128, D], BF16)) for j in range(2)]
        st6 = EA(nc.sbuf_tensor("st6", [128, 48], F32))
        mv = EA(nc.sbuf_tensor("mv", [128, 16], F32))
        stdv = EA(nc.sbuf_tensor("stdv", [128, 8], F32))
        rstd = EA(nc.sbuf_tensor("rstd", [128, 8], F32))
        nm = EA(nc.sbuf_tensor("nm", [128, 8], F32))
        epst = EA(nc.sbuf_tensor("epst", [128, 1], F32))
        ones_bf = EA(nc.sbuf_tensor("onesbf", [128, 128], BF16))
        ident_bf = EA(nc.sbuf_tensor("identbf", [128, 128], BF16))
        b1c = EA(nc.sbuf_tensor("b1csb", [128, 128], F32))
        gate_wT_bf = EA(nc.sbuf_tensor("gwtbf", [128, 64], BF16))
        gateb_bf = EA(nc.sbuf_tensor("gbbf", [1, NE], BF16))
        b2bf = EA(nc.sbuf_tensor("b2bf", [8, D], BF16))

        block = E(nc.Block())
        env = Env(nc, block, ctx)
        g = {}

        def setup_gp(gp):
            scg = env.S("cg")
            for dst, src in ((ones_bf, ones_d), (ident_bf, ident_d),
                             (gate_wT_bf, gwt_d), (gateb_bf, gb_d),
                             (b2bf, b2_d)):
                ins = gp.dma_start(out=dst, in_=src)
                scg.inc(ins, 16)
            g["consts"] = (scg.h, scg.n)
        env.gp(setup_gp)

        def setup_sync(sync):
            scx = env.S("cx")
            ins = sync.dma_start(out=b1c, in_=b1c_d)
            scx.inc(ins, 16)
            g["consts_x"] = (scx.h, scx.n)
        env.sync(setup_sync)

        def setup_dve(v):
            ins = v.memset(epst, 1e-5)
            env.sDVE.inc(ins)
        env.dve(setup_dve)

        cw, cx, cd = g["consts"], g["consts_x"], env.sDVE.n

        def wait_consts(e):
            e.wait_ge(cw[0], cw[1])
            e.wait_ge(cx[0], cx[1])
            e.wait_ge(env.sDVE.h, cd)
        for sec in (env.pe, env.act, env.dve):
            sec(wait_consts)

        base = {
            "xn_fm": xn_fm, "xt": xt, "xn_tm": xn_tm,
            "stats": (st6, mv, stdv, rstd, nm), "ident_bf": ident_bf,
            "eps_ap": epst, "ones_bf": ones_bf, "b1c": b1c,
            "gate_wT_bf": gate_wT_bf, "gateb_bf": gateb_bf, "b2bf": b2bf,
        }

        chain = []
        if "ffn1" in phases:
            dst = x1_d if ("attn" in phases or "ffn2" in phases) else out_d
            chain.append(("ffn", x_in, dst))
        if "attn" in phases:
            src = x1_d if "ffn1" in phases else x_in
            chain.append(("attn", src, x2_d if "ffn2" in phases else out_d))
        if "ffn2" in phases:
            src = x2_d if "attn" in phases else (x1_d if "ffn1" in phases else x_in)
            chain.append(("ffn", src, out_d))

        for ph_i, (kind, src, dst) in enumerate(chain):
            if kind == "ffn":
                with ExitStack() as fctx:
                    def FA(cm):
                        return fctx.enter_context(cm).ap()
                    ten = dict(base)
                    ten.update({
                        "w1bf": [FA(nc.sbuf_tensor(f"p{ph_i}w1bf{dc}", [128, HFF], BF16)) for dc in range(DC)],
                        "w2bf": [FA(nc.sbuf_tensor(f"p{ph_i}w2bf{hc}", [128, D], BF16)) for hc in range(NHC)],
                        "wselbf": [FA(nc.sbuf_tensor(f"p{ph_i}wbf{j}", [128, 8], BF16)) for j in range(2)],
                        "wselT": FA(nc.sbuf_tensor(f"p{ph_i}wselT", [8, 2048], BF16)),
                        "hsb": [FA(nc.sbuf_tensor(f"p{ph_i}hsb{hc}", [128, 512], BF16)) for hc in range(NHC)],
                        "accum": FA(nc.sbuf_tensor(f"p{ph_i}accum", [128, 16 * D], F32)),
                        "wsel": FA(nc.sbuf_tensor(f"p{ph_i}wsel", [128, 128], F32)),
                        "gate_bufs": (
                            FA(nc.sbuf_tensor(f"p{ph_i}lgall", [128, 128], F32)),
                            FA(nc.sbuf_tensor(f"p{ph_i}mx8all", [128, 128], F32)),
                            FA(nc.sbuf_tensor(f"p{ph_i}e8all", [128, 128], F32)),
                            FA(nc.sbuf_tensor(f"p{ph_i}sume", [128, 16], F32)),
                            FA(nc.sbuf_tensor(f"p{ph_i}rsum", [128, 16], F32)),
                        ),
                        "psum_h": [FA(nc.psum_tensor(f"p{ph_i}psh{j}", [128, 512], F32)) for j in range(3)],
                        "psum_og": [FA(nc.psum_tensor(f"p{ph_i}psog{j}", [128, 512], F32)) for j in range(2)],
                        "psum_tr": [FA(nc.psum_tensor(f"p{ph_i}pstr{j}", [128, 1024], BF16)) for j in range(2)],
                        "psum_misc": FA(nc.psum_tensor(f"p{ph_i}psmisc", [128, 512], F32)),
                    })
                    emit_ffn(env, g, T, src, dst, w1_d, w2_d, b2_d, ten)
                    env.barrier()
            else:
                with ExitStack() as actx:
                    def AA(cm):
                        return actx.enter_context(cm).ap()
                    nts = N // 128
                    ten = dict(base)
                    ten.update({
                        "qkvb_bf": AA(nc.sbuf_tensor("qkvbbf", [1, 768], BF16)),
                        "projb_bf": AA(nc.sbuf_tensor("projbbf", [1, D], BF16)),
                        "qk_lin": [AA(nc.sbuf_tensor(f"qkl{i}", [128, 512], BF16)) for i in range(nts)],
                        "v_aug": [AA(nc.sbuf_tensor(f"vaug{i}", [128, 4 * VB], BF16)) for i in range(nts)],
                        "qf": [AA(nc.sbuf_tensor(f"qf{j}", [128, N], BF16)) for j in range(2)],
                        "kf": [AA(nc.sbuf_tensor(f"kf{j}", [128, N], BF16)) for j in range(2)],
                        "PT": [AA(nc.sbuf_tensor(f"ptw{i}", [128, 1024], BF16)) for i in range(nts)],
                        "avfm": [AA(nc.sbuf_tensor(f"avfm{j}", [128, N], BF16)) for j in range(DC)],
                        "qkvw_bf": [AA(nc.sbuf_tensor(f"qkvwbf{dc}", [128, 768], BF16)) for dc in range(DC)],
                        "projw_bf": [AA(nc.sbuf_tensor(f"pjwbf{dc}", [128, D], BF16)) for dc in range(DC)],
                        "cos4": AA(nc.sbuf_tensor("cos4sb", [128, N], BF16)),
                        "sin4": AA(nc.sbuf_tensor("sin4sb", [128, N], BF16)),
                        "ropet": [[AA(nc.sbuf_tensor(f"ropet{sj}_{k}", [128, 128], BF16)) for k in range(4)] for sj in range(3)],
                        "rrow": AA(nc.sbuf_tensor("rrow", [1, 2048], BF16)),
                        "rb_sb": AA(nc.sbuf_tensor("rbsb", [128, 512], BF16)),
                        "psum_st": [AA(nc.psum_tensor("psstq", [128, 2048], F32))],
                        "psum_av": [AA(nc.psum_tensor(f"psav{hj}", [128, 512], F32)) for hj in range(2)],
                        "psum_rb": AA(nc.psum_tensor("psrb", [128, 512], F32)),
                        "psum_tr": [AA(nc.psum_tensor("psatr0", [128, 1024], BF16))],
                    })

                    def cs_f(gp):
                        scs = env.S("cs")
                        ins = gp.dma_start(out=ten["cos4"], in_=cos4_d)
                        scs.inc(ins, 16)
                        ins = gp.dma_start(out=ten["sin4"], in_=sin4_d)
                        scs.inc(ins, 16)
                        ins = gp.dma_start(out=ten["projb_bf"], in_=projb_d)
                        scs.inc(ins, 16)
                        g["cs"] = (scs.h, scs.n)
                    env.gp(cs_f)

                    def cs_wait(v):
                        wtup(v, g, "cs")
                    env.dve(cs_wait)

                    emit_attn(env, g, S, N, src, dst, qkvwT_d, projwT_d, ten,
                              qkvb_d, (dbg_d, dbg2_d) if dbg_d is not None else None)
                    env.barrier()

        env.barrier()

    return nc


def make_host_inputs(inp, S=2, N=2048, n_cores=8):
    x = np.asarray(inp["L0_attn_x"], dtype=np.float32)
    qkv_w = np.asarray(inp["L0_attn_qkv_w"], dtype=np.float32)
    qkv_b = np.asarray(inp["L0_attn_qkv_b"], dtype=np.float32)
    proj_w = np.asarray(inp["L0_attn_proj_w"], dtype=np.float32)
    proj_b = np.asarray(inp["L0_attn_proj_b"], dtype=np.float32)
    gate_w = np.asarray(inp["L0_ffn_gate_w"], dtype=np.float32)
    gate_b = np.asarray(inp["L0_ffn_gate_b"], dtype=np.float32)
    w1 = np.asarray(inp["L0_ffn_w1"], dtype=np.float32)
    b1 = np.asarray(inp["L0_ffn_b1"], dtype=np.float32)
    w2 = np.asarray(inp["L0_ffn_w2"], dtype=np.float32)
    b2 = np.asarray(inp["L0_ffn_b2"], dtype=np.float32)

    nts = N // 128
    pos = np.arange(N, dtype=np.float32)
    inv = (1.0 / ROPE_BASE ** (np.arange(0, HD, 2, dtype=np.float32) / np.float32(HD))).astype(np.float32)
    freqs = (pos[:, None] * inv[None, :]).astype(np.float32)
    cos4 = np.tile(np.cos(freqs).astype(np.float32), (1, 4))
    sin4 = np.tile(np.sin(freqs).astype(np.float32), (1, 4))
    cos4_l = np.ascontiguousarray(cos4.reshape(nts, 128, 128).transpose(1, 0, 2).reshape(128, N))
    sin4_l = np.ascontiguousarray(sin4.reshape(nts, 128, 128).transpose(1, 0, 2).reshape(128, N))

    common = dict(
        w1=np.ascontiguousarray(w1),
        w2=np.ascontiguousarray(w2),
        b1c=np.ascontiguousarray(b1.reshape(NE, NHC, 128).transpose(2, 0, 1).reshape(128, 128)),
        b2=np.ascontiguousarray(b2),
        gwt=np.ascontiguousarray(gate_w.T.reshape(NE, 128, NE).transpose(1, 0, 2).reshape(128, 64)),
        gb=np.ascontiguousarray(gate_b[None, :]),
        qkvwT=np.ascontiguousarray(qkv_w.T),
        qkvb=np.ascontiguousarray(qkv_b[None, :]),
        projwT=np.ascontiguousarray(proj_w.T),
        projb=np.ascontiguousarray(proj_b[None, :]),
        cos4=cos4_l,
        sin4=sin4_l,
        onesd=np.ones((128, 128), dtype=np.float32),
        identd=np.eye(128, dtype=np.float32),
    )
    maps = []
    B = x.shape[0]
    per = B // n_cores
    for c in range(n_cores):
        xs = np.ascontiguousarray(
            x[c * per:(c + 1) * per].reshape(per * x.shape[1], x.shape[2]))
        maps.append(dict(common, x=xs))
    return maps


def kernel(**inputs):
    S, N = 2, 2048
    nc = build_program(S=S, N=N)
    in_maps = make_host_inputs(inputs, S=S, N=N, n_cores=8)
    res = run_bass_kernel_spmd(nc, in_maps, core_ids=list(range(8)))
    outs = [res.results[c]["out"].reshape(S, N, D) for c in range(8)]
    return np.concatenate(outs, axis=0).astype(np.float32)


# version 16
# speedup vs baseline: 1.2334x; 1.0828x over previous
import numpy as np
import ml_dtypes
import concourse.bass as bass
import concourse.mybir as mybir
from concourse.bass_utils import run_bass_kernel_spmd
from contextlib import ExitStack

F32 = mybir.dt.float32
BF16 = mybir.dt.bfloat16
AF = mybir.ActivationFunctionType
OP = mybir.AluOpType
AX = mybir.AxisListType

D = 1024
DC = D // 128
NE = 8
HFF = 2048
NHC = HFF // 128
NH = 16
HD = 64
HG = 4
VB = 68
NXT = 3
ROPE_BASE = 10000.0
DEBUG_AVFM = False


def wtup(eng, g, key):
    t = g.get(key)
    if t is not None:
        eng.wait_ge(t[0], t[1])


class Sem:
    def __init__(self, ctx, nc, name):
        self.h = ctx.enter_context(nc.semaphore(name))
        self.n = 0

    def inc(self, instr, k=1):
        instr.then_inc(self.h, k)
        self.n += k
        return self.n

    def wait(self, eng, n=None):
        eng.wait_ge(self.h, self.n if n is None else n)


class Env:
    def __init__(self, nc, block, ctx):
        self.nc = nc
        self.block = block
        self.ctx = ctx
        self._sems = {}
        self.sPE = self.S("sPE")
        self.sACT = self.S("sACT")
        self.sDVE = self.S("sDVE")

    def S(self, name):
        if name not in self._sems:
            self._sems[name] = Sem(self.ctx, self.nc, name)
        return self._sems[name]

    def pe(self, fn):
        self.block.tensor(fn)

    def act(self, fn):
        self.block.scalar(fn)

    def dve(self, fn):
        self.block.vector(fn)

    def sync(self, fn):
        self.block.sync(fn)

    def gp(self, fn):
        self.block.gpsimd(fn)

    def barrier(self):
        tgts = [(s.h, s.n) for s in self._sems.values()]

        def mk(eng):
            for h, n in tgts:
                eng.wait_ge(h, n)

        for sec in (self.pe, self.act, self.dve, self.sync, self.gp):
            sec(mk)


def emit_ln_block(env, g, src_dram, row0, ntiles, ten, extra_cb=None):
    xn_fm, xt, xn_tm = ten["xn_fm"], ten["xt"], ten["xn_tm"]
    st6, mv, stdv, rstd, nm = ten["stats"]
    psum_tr, ident_bf, eps_ap = ten["psum_tr"], ten["ident_bf"], ten["eps_ap"]
    for p in range(0, ntiles, 2):
        pair = [p, p + 1] if p + 1 < ntiles else [p]
        pq = (p // 2) % 4

        def loads(sync, pair=pair):
            for j in pair:
                sync.wait_ge(env.sDVE.h, g.get(("xtfree", j % NXT), 0))
                wtup(sync, g, ("xtfreeO", j % NXT))
                sx = env.S(f"xt{j % NXT}")
                ins = sync.dma_start(
                    out=xt[j % NXT],
                    in_=src_dram[(row0 + j) * 128:(row0 + j + 1) * 128, :])
                sx.inc(ins, 16)
                g[("xtload", j % NXT)] = (sx.h, sx.n)
        env.sync(loads)

        def stats_f(v, pair=pair, pq=pq):
            v.wait_ge(env.sACT.h, g.get(("ln_act", pq), 0))
            for idx, j in enumerate(pair):
                wtup(v, g, ("xtload", j % NXT))
                so = 24 * (pq % 2) + 12 * idx
                v.bn_stats(out=st6[:, so:so + 6], in_=xt[j % NXT][:, 0:512])
                ins = v.bn_stats(out=st6[:, so + 6:so + 12],
                                 in_=xt[j % NXT][:, 512:1024])
                env.sDVE.inc(ins)
            v.wait_ge(env.sDVE.h, env.sDVE.n)
            for idx, j in enumerate(pair):
                so = 24 * (pq % 2) + 12 * idx
                ins = v.bn_aggr(out=mv[:, 4 * pq + 2 * idx:4 * pq + 2 * idx + 2],
                                in_=st6[:, so:so + 12])
                env.sDVE.inc(ins)
            g[("ln_stats", pq)] = env.sDVE.n
        env.dve(stats_f)

        def sqrt_f(s, pair=pair, pq=pq):
            s.wait_ge(env.sDVE.h, g[("ln_stats", pq)])
            ins = s.activation(out=stdv[:, 2 * pq:2 * pq + len(pair)],
                               in_=mv[:, 4 * pq + 1:4 * pq + 2 * len(pair):2],
                               func=AF.Sqrt, bias=eps_ap)
            env.sACT.inc(ins)
            g[("ln_act", pq)] = env.sACT.n
        env.act(sqrt_f)

        def norm_f(v, pair=pair, pq=pq, p=p):
            v.wait_ge(env.sACT.h, g[("ln_act", pq)])
            ins = v.reciprocal(out=rstd[:, 2 * pq:2 * pq + len(pair)],
                               in_=stdv[:, 2 * pq:2 * pq + len(pair)])
            env.sDVE.inc(ins)
            ins = v.tensor_scalar(out=nm[:, 2 * pq:2 * pq + len(pair)],
                                  in0=mv[:, 4 * pq:4 * pq + 2 * len(pair):2],
                                  scalar1=-1.0, scalar2=None, op0=OP.mult)
            env.sDVE.inc(ins)
            v.wait_ge(env.sDVE.h, env.sDVE.n)
            for idx, j in enumerate(pair):
                v.wait_ge(env.sPE.h, g.get(("xntm_war", j % 2), 0))
                ins = v.tensor_scalar(out=xn_tm[j % 2], in0=xt[j % NXT],
                                      scalar1=nm[:, 2 * pq + idx:2 * pq + idx + 1],
                                      scalar2=rstd[:, 2 * pq + idx:2 * pq + idx + 1],
                                      op0=OP.add, op1=OP.mult)
                env.sDVE.inc(ins)
                g[("xtfree", j % NXT)] = env.sDVE.n
                g[("xn_tm", j % 2)] = env.sDVE.n
        env.dve(norm_f)

        ntr = len(psum_tr)
        for j in pair:
            def tr_f(t, j=j):
                t.wait_ge(env.sDVE.h, g[("xn_tm", j % 2)])
                tf = g.get(("trfree", j % ntr), 0)
                if isinstance(tf, tuple):
                    t.wait_ge(env.sACT.h, tf[1])
                else:
                    t.wait_ge(env.sDVE.h, tf)
                for dc in range(DC):
                    ins = t.transpose(psum_tr[j % ntr][:, 128 * dc:128 * (dc + 1)],
                                      xn_tm[j % 2][:, 128 * dc:128 * (dc + 1)],
                                      ident_bf)
                env.sPE.inc(ins)
                g[("trdone", j % ntr)] = env.sPE.n
                g[("xntm_war", j % 2)] = env.sPE.n
            env.pe(tr_f)

            def trev_f(v, j=j):
                v.wait_ge(env.sPE.h, g[("trdone", j % ntr)])
                for dc in range(DC):
                    ins = v.tensor_copy(out=xn_fm[dc][:, 128 * j:128 * (j + 1)],
                                        in_=psum_tr[j % ntr][:, 128 * dc:128 * (dc + 1)])
                env.sDVE.inc(ins)
                g[("trfree", j % ntr)] = env.sDVE.n
            env.dve(trev_f)
        if extra_cb is not None:
            extra_cb(p)
    g["xn_ready"] = env.sDVE.n


def emit_gate(env, g, ntiles, ten):
    xn_fm, ones_bf = ten["xn_fm"], ten["ones_bf"]
    gate_wT_bf, gateb_bf = ten["gate_wT_bf"], ten["gateb_bf"]
    psum_misc, wsel = ten["psum_misc"], ten["wsel"]
    lgall, mx8all, e8all, sume, rsum = ten["gate_bufs"]

    for i in range(ntiles):
        def pe_i(t, i=i):
            if i == 0:
                t.wait_ge(env.sDVE.h, g["xn_ready"])
            else:
                t.wait_ge(env.sDVE.h, g[("gate_evac", i - 1)])
            for dc in range(DC):
                t.matmul(psum_misc[:, 0:8], xn_fm[dc][:, 128 * i:128 * (i + 1)],
                         gate_wT_bf[:, 8 * dc:8 * (dc + 1)],
                         start=(dc == 0), stop=False)
            ins = t.matmul(psum_misc[:, 0:8], ones_bf[0:1, 0:128],
                           gateb_bf[0:1, 0:8], start=False, stop=True)
            env.sPE.inc(ins)
            g[("gate_mm", i)] = env.sPE.n
        env.pe(pe_i)

        def dve_i(v, i=i):
            v.wait_ge(env.sPE.h, g[("gate_mm", i)])
            ins = v.tensor_copy(out=lgall[:, 8 * i:8 * (i + 1)], in_=psum_misc[:, 0:8])
            env.sDVE.inc(ins)
            g[("gate_evac", i)] = env.sDVE.n
        env.dve(dve_i)

    def dve_a(v):
        v.wait_ge(env.sDVE.h, g[("gate_evac", ntiles - 1)])
        for i in range(ntiles):
            ins = v.max(out=mx8all[:, 8 * i:8 * (i + 1)], in_=lgall[:, 8 * i:8 * (i + 1)])
        env.sDVE.inc(ins)
        v.wait_ge(env.sDVE.h, env.sDVE.n)
        ins = v.tensor_scalar(out=rsum[:, 0:ntiles], in0=mx8all[:, 0:8 * ntiles:8],
                              scalar1=-1.0, scalar2=None, op0=OP.mult)
        env.sDVE.inc(ins)
        g["gate_m1"] = env.sDVE.n
    env.dve(dve_a)

    def act_a(s):
        s.wait_ge(env.sDVE.h, g["gate_m1"])
        for i in range(ntiles):
            ins = s.activation(out=e8all[:, 8 * i:8 * (i + 1)],
                               in_=lgall[:, 8 * i:8 * (i + 1)], func=AF.Exp,
                               bias=rsum[:, i:i + 1], accum_out=sume[:, i:i + 1])
        env.sACT.inc(ins)
        g["gate_exp"] = env.sACT.n
    env.act(act_a)

    def dve_b(v, ntiles=ntiles):
        v.wait_ge(env.sACT.h, g["gate_exp"])
        ins = v.reciprocal(out=rsum[:, 0:ntiles], in_=sume[:, 0:ntiles])
        env.sDVE.inc(ins)
        for i in range(ntiles):
            ins = v.tensor_scalar(out=wsel[:, 8 * i:8 * (i + 1)],
                                  in0=lgall[:, 8 * i:8 * (i + 1)],
                                  scalar1=mx8all[:, 8 * i + 1:8 * i + 2], scalar2=None,
                                  op0=OP.is_ge)
        env.sDVE.inc(ins)
        v.wait_ge(env.sDVE.h, env.sDVE.n)
        ins = v.tensor_tensor(out=wsel[:, 0:8 * ntiles], in0=wsel[:, 0:8 * ntiles],
                              in1=e8all[:, 0:8 * ntiles], op=OP.mult)
        env.sDVE.inc(ins)
        v.wait_ge(env.sDVE.h, env.sDVE.n)
        for i in range(ntiles):
            ins = v.tensor_scalar(out=wsel[:, 8 * i:8 * (i + 1)],
                                  in0=wsel[:, 8 * i:8 * (i + 1)],
                                  scalar1=rsum[:, i:i + 1], scalar2=None, op0=OP.mult)
        env.sDVE.inc(ins)
        g["wsel_ready"] = env.sDVE.n
    env.dve(dve_b)

    wselbf, wselT = ten["wselbf"], ten["wselT"]
    for i in range(ntiles):
        def dve_c1(v, i=i):
            if i == 0:
                v.wait_ge(env.sDVE.h, g["wsel_ready"])
            v.wait_ge(env.sPE.h, g.get(("wbf_war", i % 2), 0))
            ins = v.tensor_copy(out=wselbf[i % 2], in_=wsel[:, 8 * i:8 * (i + 1)])
            env.sDVE.inc(ins)
            g[("wbf", i % 2)] = env.sDVE.n
        env.dve(dve_c1)

        def pe_c(t, i=i):
            t.wait_ge(env.sDVE.h, g[("wbf", i % 2)])
            t.wait_ge(env.sDVE.h, g.get(("trfree", i % 2), 0))
            ins = t.transpose(ten["psum_tr"][i % 2][0:8, 0:128], wselbf[i % 2],
                              ten["ident_bf"])
            env.sPE.inc(ins)
            g[("wbf_war", i % 2)] = env.sPE.n
            g[("wTt", i % 2)] = env.sPE.n
        env.pe(pe_c)

        def dve_c2(v, i=i):
            v.wait_ge(env.sPE.h, g[("wTt", i % 2)])
            ins = v.tensor_copy(out=wselT[0:8, 128 * i:128 * (i + 1)],
                                in_=ten["psum_tr"][i % 2][0:8, 0:128])
            env.sDVE.inc(ins)
            g[("trfree", i % 2)] = env.sDVE.n
            g["wselT_rdy"] = env.sDVE.n
        env.dve(dve_c2)


def _emit_b2mix_mm(env, t, g, i, ten, ntiles, dvebase, ilocal):
    if i >= ntiles:
        return
    for oh in range(2):
        if ilocal == 0:
            t.wait_ge(env.sDVE.h, g.get(("ogfree", oh), 0))
        else:
            t.wait_ge(env.sDVE.h, dvebase + 2 * ilocal)
        ins = t.matmul(ten["psum_og"][oh][:, 0:512],
                       ten["wselT"][0:8, 128 * i:128 * (i + 1)],
                       ten["b2bf"][0:8, 512 * oh:512 * (oh + 1)],
                       start=True, stop=True)
        env.sPE.inc(ins)
    g[("b2mm", i)] = env.sPE.n


def _emit_b2mix_evac(env, g, i, ten, ntiles):
    if i >= ntiles:
        return

    def dve_f(v, i=i):
        v.wait_ge(env.sPE.h, g[("b2mm", i)])
        for oh in range(2):
            col = 1024 * i + 512 * oh
            ins = v.tensor_copy(out=ten["accum"][:, col:col + 512],
                                in_=ten["psum_og"][oh][:, 0:512])
            env.sDVE.inc(ins)
            g[("ogfree", oh)] = env.sDVE.n
        g[("accum_init", i)] = env.sDVE.n
    env.dve(dve_f)


def emit_ffn_weight_load(env, g, e, w1_dram, w2_dram, b2_dram, ten):
    w1bf, w2bf = ten["w1bf"], ten["w2bf"]

    def gp_f(gp, e=e):
        sw1, sw2 = env.S("w1"), env.S("w2")
        gp.wait_ge(env.sPE.h, g.get("w1_inuse", 0))
        for dc in range(DC):
            ins = gp.dma_start(out=w1bf[dc],
                               in_=w1_dram[e, 128 * dc:128 * (dc + 1), :])
            sw1.inc(ins, 16)
        g["w1dma"] = (sw1.h, sw1.n)
        gp.wait_ge(env.sPE.h, g.get("w2_inuse", 0))
        for hc in range(NHC):
            ins = gp.dma_start(out=w2bf[hc],
                               in_=w2_dram[e, 128 * hc:128 * (hc + 1), :])
            sw2.inc(ins, 16)
        g["w2dma"] = (sw2.h, sw2.n)
    env.gp(gp_f)


def emit_ffn_block(env, g, row0, ntiles, src_dram, dst_dram, w1_dram, w2_dram,
                   b2_dram, ten):
    wbctx = g.pop("pending_wb", None)

    def wb_cb(p):
        if wbctx is not None:
            for i in (2 * (p // 2), 2 * (p // 2) + 1):
                if i < wbctx["ntiles"]:
                    emit_wb_tile(env, g, wbctx, i, ten)

    emit_ln_block(env, g, src_dram, row0, ntiles, ten, extra_cb=wb_cb)
    if wbctx is not None:
        for i in (2 * (ntiles // 2), 2 * (ntiles // 2) + 1):
            pass
    emit_gate(env, g, ntiles, ten)

    w1bf, w2bf = ten["w1bf"], ten["w2bf"]
    hsb, accum, wsel = ten["hsb"], ten["accum"], ten["wsel"]
    psum_h, psum_og = ten["psum_h"], ten["psum_og"]
    b1c, ones_bf, xn_fm, xt = ten["b1c"], ten["ones_bf"], ten["xn_fm"], ten["xt"]
    NA = (ntiles * 128) // 512

    for e in range(NE):
        w1_tgt, w2_tgt = g["w1dma"], g["w2dma"]

        for tj in range(NA):
            act_base = env.sACT.n
            b2_dvebase = env.sDVE.n

            def pe_a(t, e=e, tj=tj, w1_tgt=w1_tgt, act_base=act_base,
                     ntiles=ntiles, b2_dvebase=b2_dvebase):
                t.wait_ge(w1_tgt[0], w1_tgt[1])
                if e == 0:
                    t.wait_ge(env.sDVE.h, g["wselT_rdy"])
                for hc in range(NHC):
                    pb = hc % 3
                    if hc >= 3:
                        t.wait_ge(env.sACT.h, act_base + hc - 2)
                    else:
                        t.wait_ge(env.sACT.h, g.get(("hfree", pb), 0))
                    for dc in range(DC):
                        ins = t.matmul(psum_h[pb][:, 0:512],
                                       w1bf[dc][:, 128 * hc:128 * (hc + 1)],
                                       xn_fm[dc][:, 512 * tj:512 * (tj + 1)],
                                       start=(dc == 0), stop=(dc == DC - 1))
                    env.sPE.inc(ins)
                    g[("hmm", hc)] = env.sPE.n
                    if e == 0 and tj < 2 and hc % 2 == 1:
                        _emit_b2mix_mm(env, t, g, 8 * tj + hc // 2, ten, ntiles,
                                       b2_dvebase, hc // 2)
            env.pe(pe_a)
            g["w1_lastA"] = env.sPE.n
            if e == 0:
                if tj == 0:
                    for i in range(0, min(8, ntiles)):
                        _emit_b2mix_evac(env, g, i, ten, ntiles)
                elif tj == 1:
                    for i in range(8, ntiles):
                        _emit_b2mix_evac(env, g, i, ten, ntiles)

            def act_a(s, e=e):
                for hc in range(NHC):
                    pb = hc % 3
                    s.wait_ge(env.sPE.h, g[("hmm", hc)])
                    s.wait_ge(env.sPE.h, g.get("h_consumed", 0))
                    ins = s.activation(out=hsb[hc], in_=psum_h[pb], func=AF.Relu,
                                       bias=b1c[:, 16 * e + hc:16 * e + hc + 1])
                    env.sACT.inc(ins)
                    g[("hfree", pb)] = env.sACT.n
                g["hrdy"] = env.sACT.n
            env.act(act_a)
            hrdy = None

            dve_base = env.sDVE.n

            def pe_b(t, e=e, tj=tj, w2_tgt=w2_tgt, dve_base=dve_base):
                t.wait_ge(w2_tgt[0], w2_tgt[1])
                t.wait_ge(env.sACT.h, g["hrdy"])
                for t128 in range(4):
                    for oh in range(2):
                        k = t128 * 2 + oh
                        ob = k % 2
                        if k >= 2:
                            t.wait_ge(env.sDVE.h, dve_base + k - 1)
                        else:
                            t.wait_ge(env.sDVE.h, g.get(("ogfree", ob), 0))
                        for hc in range(NHC):
                            ins = t.matmul(psum_og[ob][:, 0:512],
                                           hsb[hc][:, 128 * t128:128 * (t128 + 1)],
                                           w2bf[hc][:, 512 * oh:512 * (oh + 1)],
                                           start=(hc == 0), stop=(hc == NHC - 1))
                        env.sPE.inc(ins)
                        g.setdefault("ogq", []).append(
                            (ob, env.sPE.n, 4 * tj + t128, oh))
                g["h_consumed"] = env.sPE.n
            env.pe(pe_b)
            g["w2_lastB"] = env.sPE.n

            def dve_bb(v, e=e, ntiles=ntiles):
                v.wait_ge(env.sDVE.h, g["wsel_ready"])
                if e == 0:
                    v.wait_ge(env.sDVE.h, g.get(("accum_init", ntiles - 1), 0))
                q = g.pop("ogq", [])
                for (ob, pecount, i, oh) in q:
                    v.wait_ge(env.sPE.h, pecount)
                    col = 1024 * i + 512 * oh
                    ins = v.scalar_tensor_tensor(
                        out=accum[:, col:col + 512], in0=psum_og[ob][:, 0:512],
                        scalar=wsel[:, 8 * i + e:8 * i + e + 1],
                        in1=accum[:, col:col + 512], op0=OP.mult, op1=OP.add)
                    env.sDVE.inc(ins)
                    g[("ogfree", ob)] = env.sDVE.n
                    g[("accum_last", i, oh)] = env.sDVE.n
            env.dve(dve_bb)

        if e + 1 < NE:
            g["w1_inuse"] = g["w1_lastA"]
            g["w2_inuse"] = g["w2_lastB"]
            emit_ffn_weight_load(env, g, e + 1, w1_dram, w2_dram, b2_dram, ten)

    wbctx = dict(src=src_dram, dst=dst_dram, row0=row0, ntiles=ntiles,
                 acc_keys={(i, oh): g[("accum_last", i, oh)]
                           for i in range(ntiles) for oh in range(2)})
    g["pending_wb"] = wbctx


def emit_wb_tile(env, g, wbctx, i, ten):
    xt, accum = ten["xt"], ten["accum"]
    row0, src_dram, dst_dram = wbctx["row0"], wbctx["src"], wbctx["dst"]

    def ld_i(sync, i=i):
        sync.wait_ge(env.sDVE.h, g.get(("xtfree", i % NXT), 0))
        wtup(sync, g, ("xtfreeO", i % NXT))
        sx = env.S(f"xt{i % NXT}")
        ins = sync.dma_start(
            out=xt[i % NXT],
            in_=src_dram[(row0 + i) * 128:(row0 + i + 1) * 128, :])
        sx.inc(ins, 16)
        g[("xtload", i % NXT)] = (sx.h, sx.n)
    env.sync(ld_i)

    def wb_i(v, i=i):
        wtup(v, g, ("xtload", i % NXT))
        v.wait_ge(env.sDVE.h, wbctx["acc_keys"][(i, 1)])
        v.wait_ge(env.sDVE.h, wbctx["acc_keys"][(i, 0)])
        ins = v.scalar_tensor_tensor(
            out=xt[i % NXT], in0=accum[:, 1024 * i:1024 * (i + 1)],
            scalar=0.5, in1=xt[i % NXT], op0=OP.mult, op1=OP.add)
        env.sDVE.inc(ins)
        g[("wb", i)] = env.sDVE.n
        g[("xtfree", i % NXT)] = env.sDVE.n
    env.dve(wb_i)

    def st_i(sync, i=i):
        sync.wait_ge(env.sDVE.h, g[("wb", i)])
        so = env.S(f"so{i % NXT}")
        ins = sync.dma_start(
            out=dst_dram[(row0 + i) * 128:(row0 + i + 1) * 128, :],
            in_=xt[i % NXT])
        so.inc(ins, 16)
        g[("xtfreeO", i % NXT)] = (so.h, so.n)
    env.sync(st_i)


def flush_wb(env, g, ten):
    wbctx = g.pop("pending_wb", None)
    if wbctx is not None:
        for i in range(wbctx["ntiles"]):
            emit_wb_tile(env, g, wbctx, i, ten)


def emit_ffn(env, g, T, src_dram, dst_dram, w1_dram, w2_dram, b2_dram, ten):
    nTB = (T + 2047) // 2048
    for tb in range(nTB):
        ntiles = min(16, T // 128 - tb * 16)
        emit_ffn_weight_load(env, g, 0, w1_dram, w2_dram, b2_dram, ten)
        emit_ffn_block(env, g, tb * 16, ntiles, src_dram, dst_dram,
                       w1_dram, w2_dram, b2_dram, ten)
        g["w1_inuse"] = env.sPE.n
        g["w2_inuse"] = env.sPE.n
    flush_wb(env, g, ten)


def emit_attn(env, g, S, N, src_dram, dst_dram, qkvwT_dram, projwT_dram, ten,
              qkvb_dram=None, dbg_d=None):
    nts = N // 128
    ntq = N // 512
    xn_fm, xt = ten["xn_fm"], ten["xt"]
    qk_lin = ten["qk_lin"]
    v_aug = ten["v_aug"]
    qf, kf = ten["qf"], ten["kf"]
    PTW = ten["PT"]
    avfm = ten["avfm"]
    qkvw_bf, projw_bf = ten["qkvw_bf"], ten["projw_bf"]
    stq = ten["psum_st"][0]
    psum_av, psum_rb = ten["psum_av"], ten["psum_rb"]
    psum_tr = ten["psum_tr"]
    ones_bf, qkvb_bf, projb_bf = ten["ones_bf"], ten["qkvb_bf"], ten["projb_bf"]
    cos4, sin4, ropet = ten["cos4"], ten["sin4"], ten["ropet"]
    rrow, rb_sb = ten["rrow"], ten["rb_sb"]

    def projw_f(gp):
        spw = env.S("pw")
        for dc in range(DC):
            ins = gp.dma_start(out=projw_bf[dc],
                               in_=projwT_dram[128 * dc:128 * (dc + 1), :])
            spw.inc(ins, 16)
        g["projw_dma"] = (spw.h, spw.n)
    env.gp(projw_f)

    for s_i in range(S):
        row0 = s_i * nts
        emit_ln_block(env, g, src_dram, row0, nts, ten)

        for hg in range(NH // HG):
            def qkvw_f(gp, hg=hg):
                sq = env.S("qkvw")
                gp.wait_ge(env.sPE.h, g.get("qkvw_inuse", 0))
                for dc in range(DC):
                    for pi, off in enumerate((0, D, 2 * D)):
                        ins = gp.dma_start(
                            out=qkvw_bf[dc][:, 256 * pi:256 * (pi + 1)],
                            in_=qkvwT_dram[128 * dc:128 * (dc + 1),
                                           off + 256 * hg:off + 256 * (hg + 1)])
                        sq.inc(ins, 16)
                for pi, off in enumerate((0, D, 2 * D)):
                    ins = gp.dma_start(
                        out=qkvb_bf[0:1, 256 * pi:256 * (pi + 1)],
                        in_=qkvb_dram[0:1, off + 256 * hg:off + 256 * (hg + 1)])
                    sq.inc(ins, 16)
                g["qkvw_dma"] = (sq.h, sq.n)
            env.gp(qkvw_f)

            for i in range(nts):
                def pe_q(t, i=i, hg=hg):
                    wtup(t, g, "qkvw_dma")
                    wtup(t, g, "cs")
                    if i == 0:
                        t.wait_ge(env.sDVE.h, g["xn_ready"])
                    for pi in range(3):
                        t.wait_ge(env.sDVE.h, g.get(("qkvps_free", pi), 0))
                        for dc in range(DC):
                            t.matmul(stq[:, 512 * pi:512 * pi + 256],
                                     xn_fm[dc][:, 128 * i:128 * (i + 1)],
                                     qkvw_bf[dc][:, 256 * pi:256 * (pi + 1)],
                                     start=(dc == 0), stop=False)
                        ins = t.matmul(
                            stq[:, 512 * pi:512 * pi + 256], ones_bf[0:1, 0:128],
                            qkvb_bf[0:1, 256 * pi:256 * (pi + 1)],
                            start=False, stop=True)
                        env.sPE.inc(ins)
                        g[("qkvmm", pi)] = env.sPE.n
                env.pe(pe_q)

                def dve_q(v, i=i):
                    v.wait_ge(env.sPE.h, g.get("qk_consumed", 0))
                    for pi in range(3):
                        v.wait_ge(env.sPE.h, g[("qkvmm", pi)])
                        if pi < 2:
                            ins = v.tensor_copy(out=qk_lin[i][:, 256 * pi:256 * (pi + 1)],
                                                in_=stq[:, 512 * pi:512 * pi + 256])
                        else:
                            dst = v_aug[i].rearrange("p (b c) -> p b c", c=VB)[:, :, 0:64]
                            src = stq[:, 512 * pi:512 * pi + 256].rearrange("p (b c) -> p b c", c=64)
                            ins = v.tensor_copy(out=dst, in_=src)
                        env.sDVE.inc(ins)
                        g[("qkvps_free", pi)] = env.sDVE.n
                    ins = v.memset(
                        v_aug[i].rearrange("p (b c) -> p b c", c=VB)[:, :, 64:65], 1.0)
                    env.sDVE.inc(ins)
                    g[("vaug_rdy", i)] = env.sDVE.n
                env.dve(dve_q)

            g["qkvw_inuse"] = env.sPE.n

            for i in range(nts):
                for off in (0, 256):
                    def dve_r(v, i=i, off=off):
                        tset = ropet[(2 * i + off // 256) % 3]
                        e_ap = qk_lin[i][:, off + 0:off + 256:2].rearrange(
                            "p (b c) -> p b c", c=32)
                        o_ap = qk_lin[i][:, off + 1:off + 256:2].rearrange(
                            "p (b c) -> p b c", c=32)
                        c_ap = cos4[:, 128 * i:128 * (i + 1)].rearrange(
                            "p (b c) -> p b c", c=32)
                        s_ap = sin4[:, 128 * i:128 * (i + 1)].rearrange(
                            "p (b c) -> p b c", c=32)
                        t3 = [tt.rearrange("p (b c) -> p b c", c=32) for tt in tset]
                        v.tensor_tensor(out=t3[0], in0=e_ap, in1=c_ap, op=OP.mult)
                        v.tensor_tensor(out=t3[1], in0=o_ap, in1=s_ap, op=OP.mult)
                        v.tensor_tensor(out=t3[2], in0=e_ap, in1=s_ap, op=OP.mult)
                        ins = v.tensor_tensor(out=t3[3], in0=o_ap, in1=c_ap, op=OP.mult)
                        env.sDVE.inc(ins)
                        v.wait_ge(env.sDVE.h, env.sDVE.n)
                        v.tensor_tensor(out=e_ap, in0=t3[0], in1=t3[1], op=OP.subtract)
                        ins = v.tensor_tensor(out=o_ap, in0=t3[2], in1=t3[3], op=OP.add)
                        env.sDVE.inc(ins)
                        g[("rope", i, off)] = env.sDVE.n
                    env.dve(dve_r)

            for i in range(nts):
                def pe_t(t, i=i):
                    t.wait_ge(env.sDVE.h, g[("rope", i, 0)])
                    t.wait_ge(env.sDVE.h, g[("rope", i, 256)])
                    tf = g.get(("trfree", 0), 0)
                    if isinstance(tf, tuple):
                        t.wait_ge(env.sACT.h, tf[1])
                    else:
                        t.wait_ge(env.sDVE.h, tf)
                    for half in range(2):
                        t.transpose(psum_tr[0][:, 256 * half:256 * half + 128],
                                    qk_lin[i][:, 128 * half:128 * (half + 1)],
                                    ten["ident_bf"])
                        ins = t.transpose(
                            psum_tr[0][:, 256 * half + 128:256 * half + 256],
                            qk_lin[i][:, 256 + 128 * half:256 + 128 * (half + 1)],
                            ten["ident_bf"])
                    env.sPE.inc(ins)
                    g[("atr", i)] = env.sPE.n
                env.pe(pe_t)

                def act_t(a, i=i):
                    a.wait_ge(env.sPE.h, g[("atr", i)])
                    a.wait_ge(env.sPE.h, g.get("qk_consumed", 0))
                    for half in range(2):
                        a.copy(out=qf[half][:, 128 * i:128 * (i + 1)],
                               in_=psum_tr[0][:, 256 * half:256 * half + 128])
                        ins = a.copy(
                            out=kf[half][:, 128 * i:128 * (i + 1)],
                            in_=psum_tr[0][:, 256 * half + 128:256 * half + 256])
                    env.sACT.inc(ins)
                    g[("trfree", 0)] = ("ACT", env.sACT.n)
                    g[("qkfm", i)] = env.sACT.n
                env.act(act_t)

            iters = [(hp, tq) for hp in range(HG // 2) for tq in range(ntq)]
            M = len(iters)
            for m in range(M + 2):
                if m < M:
                    g[("exp_base", m)] = env.sACT.n
                def pe_main(t, m=m, hg=hg):
                    if 0 <= m - 2 < M:
                        for hi in range(2):
                            t.wait_ge(env.sDVE.h, g[("rbfcast", m - 2)])
                            t.wait_ge(env.sDVE.h, g.get(("rbevac", m - 3), 0))
                            ins = t.matmul(psum_rb[64 * hi:64 * hi + 64, 0:512],
                                           ones_bf[0:1, 0:64],
                                           rrow[0:1, 1024 * ((m - 2) % 2) + 512 * hi:
                                                1024 * ((m - 2) % 2) + 512 * hi + 512],
                                           start=True, stop=True)
                        env.sPE.inc(ins)
                        g[("rbmm", m - 2)] = env.sPE.n
                    if 0 <= m - 1 < M:
                        hp1, tq1 = iters[m - 1]
                        for hi in range(2):
                            t.wait_ge(env.sDVE.h, g.get(("avfree", hi), 0))
                        t.wait_ge(env.sDVE.h, g[("vaug_rdy", nts - 1)])
                        for tk in range(nts):
                            for hi in range(2):
                                blkoff = VB * (2 * hp1 + hi)
                                t.wait_ge(env.sACT.h, g[("exp_base", m - 1)] + tk + 1)
                                ins = t.matmul(psum_av[hi][0:65, 0:512],
                                               v_aug[tk][:, blkoff:blkoff + 65],
                                               PTW[tk][:, 512 * hi:512 * hi + 512],
                                               start=(tk == 0), stop=(tk == nts - 1))
                        env.sPE.inc(ins)
                        g[("avmm", m - 1)] = env.sPE.n
                    if m < M:
                        hp0, tq0 = iters[m]
                        act_base = g[("exp_base", m)]
                        for tk in range(nts):
                            if m == 0:
                                t.wait_ge(env.sACT.h, g[("qkfm", max(tk, 3)) if ("qkfm", max(tk, 3)) in g else ("qkfm", nts - 1)])
                            for hi in range(2):
                                k = 2 * tk + hi
                                q = k % 4
                                if k >= 4:
                                    t.wait_ge(env.sACT.h, act_base + (k - 4) // 2 + 1)
                                else:
                                    t.wait_ge(env.sACT.h, act_base)
                                ro = 64 * hi
                                ins = t.matmul(
                                    stq[:, 512 * q:512 * (q + 1)],
                                    kf[hp0][ro:ro + 64, 128 * tk:128 * (tk + 1)],
                                    qf[hp0][ro:ro + 64, 512 * tq0:512 * (tq0 + 1)],
                                    start=True, stop=True)
                                env.sPE.inc(ins)
                                g[("stmm", m, tk)] = env.sPE.n
                env.pe(pe_main)

                if m < M:
                    def act_exp(s, m=m):
                        for tk in range(nts):
                            s.wait_ge(env.sPE.h, g[("stmm", m, tk)])
                            if m >= 1:
                                s.wait_ge(env.sPE.h, g[("avmm", m - 1)])
                            ins = s.activation(out=PTW[tk],
                                               in_=stq[:, 1024 * (tk % 2):1024 * (tk % 2) + 1024],
                                               func=AF.Exp, scale=0.125)
                            env.sACT.inc(ins)
                    env.act(act_exp)

                def dve_norm(v, m=m, hg=hg):
                    if 0 <= m - 1 < M:
                        v.wait_ge(env.sPE.h, g[("avmm", m - 1)])
                        par = (m - 1) % 2
                        with env.nc.allow_low_precision(
                                reason="softmax normalizer in bf16 matches the "
                                       "bf16 P-matrix precision of this kernel"):
                            for hi in range(2):
                                ins = v.reciprocal(
                                    out=rrow[0:1, 1024 * par + 512 * hi:1024 * par + 512 * hi + 512],
                                    in_=psum_av[hi][64:65, 0:512])
                        env.sDVE.inc(ins)
                        g[("rbfcast", m - 1)] = env.sDVE.n
                        hp1, tq1 = iters[m - 1]
                        for hi in range(2):
                            hh = HG * hg + 2 * hp1 + hi
                            dst = avfm[hh // 2][64 * (hh % 2):64 * (hh % 2) + 64,
                                                512 * tq1:512 * (tq1 + 1)]
                            ins = v.tensor_copy(out=dst, in_=psum_av[hi][0:64, 0:512])
                            env.sDVE.inc(ins)
                            g[("avfree", hi)] = env.sDVE.n
                    if 0 <= m - 2 < M:
                        v.wait_ge(env.sPE.h, g[("rbmm", m - 2)])
                        v.wait_ge(env.sDVE.h, g.get(("rbmults", m - 3), 0))
                        ins = v.tensor_copy(out=rb_sb, in_=psum_rb[0:128, 0:512])
                        env.sDVE.inc(ins)
                        g[("rbevac", m - 2)] = env.sDVE.n
                        v.wait_ge(env.sDVE.h, env.sDVE.n)
                        hp2, tq2 = iters[m - 2]
                        for hi in range(2):
                            hh = HG * hg + 2 * hp2 + hi
                            dst = avfm[hh // 2][64 * (hh % 2):64 * (hh % 2) + 64,
                                                512 * tq2:512 * (tq2 + 1)]
                            ins = v.tensor_tensor(out=dst, in0=dst,
                                                  in1=rb_sb[64 * hi:64 * hi + 64, 0:512],
                                                  op=OP.mult)
                            env.sDVE.inc(ins)
                            g["avfm_last"] = env.sDVE.n
                        g[("rbmults", m - 2)] = env.sDVE.n
                env.dve(dve_norm)
            g["qk_consumed"] = env.sPE.n

        if dbg_d is not None and s_i == 0:
            dbgA, dbgB = dbg_d
            def dbg_f(gp):
                gp.wait_ge(env.sDVE.h, g["avfm_last"])
                sd = env.S("dbg")
                for j in range(DC):
                    ins = gp.dma_start(out=dbgA[128 * j:128 * (j + 1), :], in_=avfm[j])
                    sd.inc(ins, 16)
                ins = gp.dma_start(out=dbgB[0:128, :], in_=qf[0])
                sd.inc(ins, 16)
                ins = gp.dma_start(out=dbgB[128:256, :], in_=kf[0])
                sd.inc(ins, 16)
            env.gp(dbg_f)

        for i in range(nts):
            def ld_p(sync, i=i):
                sync.wait_ge(env.sDVE.h, g.get(("xtfree", i % NXT), 0))
                wtup(sync, g, ("xtfreeO", i % NXT))
                sx = env.S(f"xt{i % NXT}")
                ins = sync.dma_start(
                    out=xt[i % NXT],
                    in_=src_dram[(row0 + i) * 128:(row0 + i + 1) * 128, :])
                sx.inc(ins, 16)
                g[("xtload", i % NXT)] = (sx.h, sx.n)
            env.sync(ld_p)

            for oh in range(2):
                def pe_p(t, i=i, oh=oh):
                    if i == 0 and oh == 0:
                        t.wait_ge(env.sDVE.h, g["avfm_last"])
                        wtup(t, g, "projw_dma")
                        wtup(t, g, "cs")
                    pb = (2 * i + oh) % 2
                    t.wait_ge(env.sDVE.h, g.get(("pjfree", pb), 0))
                    t.wait_ge(env.sACT.h, g.get(("stfree", pb), 0))
                    for dc in range(DC):
                        t.matmul(stq[:, 512 * pb:512 * (pb + 1)],
                                 avfm[dc][:, 128 * i:128 * (i + 1)],
                                 projw_bf[dc][:, 512 * oh:512 * (oh + 1)],
                                 start=(dc == 0), stop=False)
                    ins = t.matmul(stq[:, 512 * pb:512 * (pb + 1)], ones_bf[0:1, 0:128],
                                   projb_bf[0:1, 512 * oh:512 * (oh + 1)],
                                   start=False, stop=True)
                    env.sPE.inc(ins)
                    g[("pjmm", pb)] = env.sPE.n
                env.pe(pe_p)

                def dve_p(v, i=i, oh=oh):
                    pb = (2 * i + oh) % 2
                    v.wait_ge(env.sPE.h, g[("pjmm", pb)])
                    wtup(v, g, ("xtload", i % NXT))
                    ins = v.tensor_tensor(
                        out=xt[i % NXT][:, 512 * oh:512 * (oh + 1)],
                        in0=stq[:, 512 * pb:512 * (pb + 1)],
                        in1=xt[i % NXT][:, 512 * oh:512 * (oh + 1)], op=OP.add)
                    env.sDVE.inc(ins)
                    g[("pjfree", pb)] = env.sDVE.n
                    g[("pwb", i)] = env.sDVE.n
                env.dve(dve_p)

            def st_p(sync, i=i):
                sync.wait_ge(env.sDVE.h, g[("pwb", i)])
                so = env.S(f"so{i % NXT}")
                ins = sync.dma_start(
                    out=dst_dram[(row0 + i) * 128:(row0 + i + 1) * 128, :],
                    in_=xt[i % NXT])
                so.inc(ins, 16)
                g[("xtfreeO", i % NXT)] = (so.h, so.n)
            env.sync(st_p)
            g[("xtfree", i % NXT)] = g[("pwb", i)]
        g["avfm_war"] = env.sPE.n
    g["qkvw_inuse"] = env.sPE.n


def build_program(S=2, N=2048, phases=("ffn1", "attn", "ffn2")):
    T = S * N
    nc = bass.Bass("TRN2", target_bir_lowering=False)

    x_in = nc.dram_tensor("x", [T, D], F32, kind="ExternalInput").ap()
    w1_d = nc.dram_tensor("w1", [NE, D, HFF], F32, kind="ExternalInput").ap()
    w2_d = nc.dram_tensor("w2", [NE, HFF, D], F32, kind="ExternalInput").ap()
    b1c_d = nc.dram_tensor("b1c", [128, 128], F32, kind="ExternalInput").ap()
    b2_d = nc.dram_tensor("b2", [NE, D], F32, kind="ExternalInput").ap()
    gwt_d = nc.dram_tensor("gwt", [128, 64], F32, kind="ExternalInput").ap()
    gb_d = nc.dram_tensor("gb", [1, NE], F32, kind="ExternalInput").ap()
    qkvwT_d = nc.dram_tensor("qkvwT", [D, 3 * D], F32, kind="ExternalInput").ap()
    qkvb_d = nc.dram_tensor("qkvb", [1, 3 * D], F32, kind="ExternalInput").ap()
    projwT_d = nc.dram_tensor("projwT", [D, D], F32, kind="ExternalInput").ap()
    projb_d = nc.dram_tensor("projb", [1, D], F32, kind="ExternalInput").ap()
    cos4_d = nc.dram_tensor("cos4", [128, N], F32, kind="ExternalInput").ap()
    sin4_d = nc.dram_tensor("sin4", [128, N], F32, kind="ExternalInput").ap()
    ones_d = nc.dram_tensor("onesd", [128, 128], F32, kind="ExternalInput").ap()
    ident_d = nc.dram_tensor("identd", [128, 128], F32, kind="ExternalInput").ap()
    out_d = nc.dram_tensor("out", [T, D], F32, kind="ExternalOutput").ap()
    dbg_d = nc.dram_tensor("dbg", [8 * 128, N], BF16, kind="ExternalOutput").ap() if DEBUG_AVFM else None
    dbg2_d = nc.dram_tensor("dbg2", [2 * 128, N], BF16, kind="ExternalOutput").ap() if DEBUG_AVFM else None
    x1_d = nc.dram_tensor("x1scratch", [T, D], F32).ap()
    x2_d = nc.dram_tensor("x2scratch", [T, D], F32).ap()

    with ExitStack() as ctx:
        E = ctx.enter_context

        def EA(cm):
            return E(cm).ap()

        xn_fm = [EA(nc.sbuf_tensor(f"xnfm{dc}", [128, min(N, 2048)], BF16)) for dc in range(DC)]
        xt = [EA(nc.sbuf_tensor(f"xt{j}", [128, D], F32)) for j in range(NXT)]
        xn_tm = [EA(nc.sbuf_tensor(f"xntm{j}", [128, D], BF16)) for j in range(2)]
        st6 = EA(nc.sbuf_tensor("st6", [128, 48], F32))
        mv = EA(nc.sbuf_tensor("mv", [128, 16], F32))
        stdv = EA(nc.sbuf_tensor("stdv", [128, 8], F32))
        rstd = EA(nc.sbuf_tensor("rstd", [128, 8], F32))
        nm = EA(nc.sbuf_tensor("nm", [128, 8], F32))
        epst = EA(nc.sbuf_tensor("epst", [128, 1], F32))
        ones_bf = EA(nc.sbuf_tensor("onesbf", [128, 128], BF16))
        ident_bf = EA(nc.sbuf_tensor("identbf", [128, 128], BF16))
        b1c = EA(nc.sbuf_tensor("b1csb", [128, 128], F32))
        gate_wT_bf = EA(nc.sbuf_tensor("gwtbf", [128, 64], BF16))
        gateb_bf = EA(nc.sbuf_tensor("gbbf", [1, NE], BF16))
        b2bf = EA(nc.sbuf_tensor("b2bf", [8, D], BF16))

        block = E(nc.Block())
        env = Env(nc, block, ctx)
        g = {}

        def setup_gp(gp):
            scg = env.S("cg")
            for dst, src in ((ones_bf, ones_d), (ident_bf, ident_d),
                             (gate_wT_bf, gwt_d), (gateb_bf, gb_d),
                             (b2bf, b2_d)):
                ins = gp.dma_start(out=dst, in_=src)
                scg.inc(ins, 16)
            g["consts"] = (scg.h, scg.n)
        env.gp(setup_gp)

        def setup_sync(sync):
            scx = env.S("cx")
            ins = sync.dma_start(out=b1c, in_=b1c_d)
            scx.inc(ins, 16)
            g["consts_x"] = (scx.h, scx.n)
        env.sync(setup_sync)

        def setup_dve(v):
            ins = v.memset(epst, 1e-5)
            env.sDVE.inc(ins)
        env.dve(setup_dve)

        cw, cx, cd = g["consts"], g["consts_x"], env.sDVE.n

        def wait_consts(e):
            e.wait_ge(cw[0], cw[1])
            e.wait_ge(cx[0], cx[1])
            e.wait_ge(env.sDVE.h, cd)
        for sec in (env.pe, env.act, env.dve):
            sec(wait_consts)

        base = {
            "xn_fm": xn_fm, "xt": xt, "xn_tm": xn_tm,
            "stats": (st6, mv, stdv, rstd, nm), "ident_bf": ident_bf,
            "eps_ap": epst, "ones_bf": ones_bf, "b1c": b1c,
            "gate_wT_bf": gate_wT_bf, "gateb_bf": gateb_bf, "b2bf": b2bf,
        }

        chain = []
        if "ffn1" in phases:
            dst = x1_d if ("attn" in phases or "ffn2" in phases) else out_d
            chain.append(("ffn", x_in, dst))
        if "attn" in phases:
            src = x1_d if "ffn1" in phases else x_in
            chain.append(("attn", src, x2_d if "ffn2" in phases else out_d))
        if "ffn2" in phases:
            src = x2_d if "attn" in phases else (x1_d if "ffn1" in phases else x_in)
            chain.append(("ffn", src, out_d))

        for ph_i, (kind, src, dst) in enumerate(chain):
            if kind == "ffn":
                with ExitStack() as fctx:
                    def FA(cm):
                        return fctx.enter_context(cm).ap()
                    ten = dict(base)
                    ten.update({
                        "w1bf": [FA(nc.sbuf_tensor(f"p{ph_i}w1bf{dc}", [128, HFF], BF16)) for dc in range(DC)],
                        "w2bf": [FA(nc.sbuf_tensor(f"p{ph_i}w2bf{hc}", [128, D], BF16)) for hc in range(NHC)],
                        "wselbf": [FA(nc.sbuf_tensor(f"p{ph_i}wbf{j}", [128, 8], BF16)) for j in range(2)],
                        "wselT": FA(nc.sbuf_tensor(f"p{ph_i}wselT", [8, 2048], BF16)),
                        "hsb": [FA(nc.sbuf_tensor(f"p{ph_i}hsb{hc}", [128, 512], BF16)) for hc in range(NHC)],
                        "accum": FA(nc.sbuf_tensor(f"p{ph_i}accum", [128, 16 * D], F32)),
                        "wsel": FA(nc.sbuf_tensor(f"p{ph_i}wsel", [128, 128], F32)),
                        "gate_bufs": (
                            FA(nc.sbuf_tensor(f"p{ph_i}lgall", [128, 128], F32)),
                            FA(nc.sbuf_tensor(f"p{ph_i}mx8all", [128, 128], F32)),
                            FA(nc.sbuf_tensor(f"p{ph_i}e8all", [128, 128], F32)),
                            FA(nc.sbuf_tensor(f"p{ph_i}sume", [128, 16], F32)),
                            FA(nc.sbuf_tensor(f"p{ph_i}rsum", [128, 16], F32)),
                        ),
                        "psum_h": [FA(nc.psum_tensor(f"p{ph_i}psh{j}", [128, 512], F32)) for j in range(3)],
                        "psum_og": [FA(nc.psum_tensor(f"p{ph_i}psog{j}", [128, 512], F32)) for j in range(2)],
                        "psum_tr": [FA(nc.psum_tensor(f"p{ph_i}pstr{j}", [128, 1024], BF16)) for j in range(2)],
                        "psum_misc": FA(nc.psum_tensor(f"p{ph_i}psmisc", [128, 512], F32)),
                    })
                    emit_ffn(env, g, T, src, dst, w1_d, w2_d, b2_d, ten)
                    env.barrier()
            else:
                with ExitStack() as actx:
                    def AA(cm):
                        return actx.enter_context(cm).ap()
                    nts = N // 128
                    ten = dict(base)
                    ten.update({
                        "qkvb_bf": AA(nc.sbuf_tensor("qkvbbf", [1, 768], BF16)),
                        "projb_bf": AA(nc.sbuf_tensor("projbbf", [1, D], BF16)),
                        "qk_lin": [AA(nc.sbuf_tensor(f"qkl{i}", [128, 512], BF16)) for i in range(nts)],
                        "v_aug": [AA(nc.sbuf_tensor(f"vaug{i}", [128, 4 * VB], BF16)) for i in range(nts)],
                        "qf": [AA(nc.sbuf_tensor(f"qf{j}", [128, N], BF16)) for j in range(2)],
                        "kf": [AA(nc.sbuf_tensor(f"kf{j}", [128, N], BF16)) for j in range(2)],
                        "PT": [AA(nc.sbuf_tensor(f"ptw{i}", [128, 1024], BF16)) for i in range(nts)],
                        "avfm": [AA(nc.sbuf_tensor(f"avfm{j}", [128, N], BF16)) for j in range(DC)],
                        "qkvw_bf": [AA(nc.sbuf_tensor(f"qkvwbf{dc}", [128, 768], BF16)) for dc in range(DC)],
                        "projw_bf": [AA(nc.sbuf_tensor(f"pjwbf{dc}", [128, D], BF16)) for dc in range(DC)],
                        "cos4": AA(nc.sbuf_tensor("cos4sb", [128, N], BF16)),
                        "sin4": AA(nc.sbuf_tensor("sin4sb", [128, N], BF16)),
                        "ropet": [[AA(nc.sbuf_tensor(f"ropet{sj}_{k}", [128, 128], BF16)) for k in range(4)] for sj in range(3)],
                        "rrow": AA(nc.sbuf_tensor("rrow", [1, 2048], BF16)),
                        "rb_sb": AA(nc.sbuf_tensor("rbsb", [128, 512], BF16)),
                        "psum_st": [AA(nc.psum_tensor("psstq", [128, 2048], F32))],
                        "psum_av": [AA(nc.psum_tensor(f"psav{hj}", [128, 512], F32)) for hj in range(2)],
                        "psum_rb": AA(nc.psum_tensor("psrb", [128, 512], F32)),
                        "psum_tr": [AA(nc.psum_tensor("psatr0", [128, 1024], BF16))],
                    })

                    def cs_f(gp):
                        scs = env.S("cs")
                        ins = gp.dma_start(out=ten["cos4"], in_=cos4_d)
                        scs.inc(ins, 16)
                        ins = gp.dma_start(out=ten["sin4"], in_=sin4_d)
                        scs.inc(ins, 16)
                        ins = gp.dma_start(out=ten["projb_bf"], in_=projb_d)
                        scs.inc(ins, 16)
                        g["cs"] = (scs.h, scs.n)
                    env.gp(cs_f)

                    def cs_wait(v):
                        wtup(v, g, "cs")
                    env.dve(cs_wait)

                    emit_attn(env, g, S, N, src, dst, qkvwT_d, projwT_d, ten,
                              qkvb_d, (dbg_d, dbg2_d) if dbg_d is not None else None)
                    env.barrier()

        env.barrier()

    return nc


def make_host_inputs(inp, S=2, N=2048, n_cores=8):
    x = np.asarray(inp["L0_attn_x"], dtype=np.float32)
    qkv_w = np.asarray(inp["L0_attn_qkv_w"], dtype=np.float32)
    qkv_b = np.asarray(inp["L0_attn_qkv_b"], dtype=np.float32)
    proj_w = np.asarray(inp["L0_attn_proj_w"], dtype=np.float32)
    proj_b = np.asarray(inp["L0_attn_proj_b"], dtype=np.float32)
    gate_w = np.asarray(inp["L0_ffn_gate_w"], dtype=np.float32)
    gate_b = np.asarray(inp["L0_ffn_gate_b"], dtype=np.float32)
    w1 = np.asarray(inp["L0_ffn_w1"], dtype=np.float32)
    b1 = np.asarray(inp["L0_ffn_b1"], dtype=np.float32)
    w2 = np.asarray(inp["L0_ffn_w2"], dtype=np.float32)
    b2 = np.asarray(inp["L0_ffn_b2"], dtype=np.float32)

    nts = N // 128
    pos = np.arange(N, dtype=np.float32)
    inv = (1.0 / ROPE_BASE ** (np.arange(0, HD, 2, dtype=np.float32) / np.float32(HD))).astype(np.float32)
    freqs = (pos[:, None] * inv[None, :]).astype(np.float32)
    cos4 = np.tile(np.cos(freqs).astype(np.float32), (1, 4))
    sin4 = np.tile(np.sin(freqs).astype(np.float32), (1, 4))
    cos4_l = np.ascontiguousarray(cos4.reshape(nts, 128, 128).transpose(1, 0, 2).reshape(128, N))
    sin4_l = np.ascontiguousarray(sin4.reshape(nts, 128, 128).transpose(1, 0, 2).reshape(128, N))

    common = dict(
        w1=np.ascontiguousarray(w1),
        w2=np.ascontiguousarray(w2),
        b1c=np.ascontiguousarray(b1.reshape(NE, NHC, 128).transpose(2, 0, 1).reshape(128, 128)),
        b2=np.ascontiguousarray(b2),
        gwt=np.ascontiguousarray(gate_w.T.reshape(NE, 128, NE).transpose(1, 0, 2).reshape(128, 64)),
        gb=np.ascontiguousarray(gate_b[None, :]),
        qkvwT=np.ascontiguousarray(qkv_w.T),
        qkvb=np.ascontiguousarray(qkv_b[None, :]),
        projwT=np.ascontiguousarray(proj_w.T),
        projb=np.ascontiguousarray(proj_b[None, :]),
        cos4=cos4_l,
        sin4=sin4_l,
        onesd=np.ones((128, 128), dtype=np.float32),
        identd=np.eye(128, dtype=np.float32),
    )
    maps = []
    B = x.shape[0]
    per = B // n_cores
    for c in range(n_cores):
        xs = np.ascontiguousarray(
            x[c * per:(c + 1) * per].reshape(per * x.shape[1], x.shape[2]))
        maps.append(dict(common, x=xs))
    return maps


def kernel(**inputs):
    S, N = 2, 2048
    nc = build_program(S=S, N=N)
    in_maps = make_host_inputs(inputs, S=S, N=N, n_cores=8)
    res = run_bass_kernel_spmd(nc, in_maps, core_ids=list(range(8)))
    outs = [res.results[c]["out"].reshape(S, N, D) for c in range(8)]
    return np.concatenate(outs, axis=0).astype(np.float32)
